# version 08
# baseline (speedup 1.0000x reference)
import math
from contextlib import ExitStack

import numpy as np
import ml_dtypes

import concourse.bass as bass
import concourse.mybir as mybir
from concourse.ap import AP
from concourse.bass_utils import run_bass_kernel_spmd

F32 = mybir.dt.float32
BF16 = mybir.dt.bfloat16
ALU = mybir.AluOpType
ACT = mybir.ActivationFunctionType
AX = mybir.AxisListType.X

D = 1024
FS = 4096
TO = 2048
H4 = 4096
NHP = 8
LN_EPS = 1e-5
QEPS = 1e-5


ENGINES = ("T", "A", "V", "SY", "G", "ST")
DMA_ENGS = {"SY", "G", "ST"}
BODY_OF = {"T": "tensor", "A": "scalar", "V": "vector",
           "SY": "sync", "G": "gpsimd", "ST": "scalar"}


class Op:
    __slots__ = ("eng", "fn", "deps", "force", "idx", "inc", "seq", "tag")

    def __init__(self, eng, fn, deps, force, tag=""):
        self.eng = eng
        self.fn = fn
        self.deps = deps
        self.force = force
        self.idx = -1
        self.inc = False
        self.seq = 0
        self.tag = tag


class Sched:
    def __init__(self):
        self.ops = []
        self.last = {}
        self.pending_fence = {}

    def add(self, eng, fn, deps=(), tag="", force=()):
        deps = [d for d in deps if d is not None]
        force = [d for d in force if d is not None]
        fence = self.pending_fence.pop(eng, None)
        if fence:
            deps = deps + [d for d in fence if d.eng != eng]
        op = Op(eng, fn, deps, force, tag)
        op.idx = len(self.ops)
        self.ops.append(op)
        self.last[eng] = op
        return op

    def fence(self):
        bar = list(self.last.values())
        for e in ENGINES:
            self.pending_fence[e] = bar

    def finalize(self):
        for op in self.ops:
            for d in op.deps:
                same_body = BODY_OF[d.eng] == BODY_OF[op.eng]
                if d.eng in DMA_ENGS or op.eng in DMA_ENGS or not same_body:
                    d.inc = True
            for d in op.force:
                d.inc = True
        cnt = {e: 0 for e in ENGINES}
        for op in self.ops:
            if op.inc:
                cnt[op.eng] += 16 if op.eng in DMA_ENGS else 1
            op.seq = cnt[op.eng]

    def emit(self, nc, sems):
        bodies = {}
        for op in self.ops:
            bodies.setdefault(BODY_OF[op.eng], []).append(op)

        def make_body(ops_for_body):
            def body(e):
                waited = {en: 0 for en in ENGINES}
                for op in ops_for_body:
                    for d in op.deps:
                        if (BODY_OF[d.eng] == BODY_OF[op.eng]
                                and d.eng not in DMA_ENGS
                                and op.eng not in DMA_ENGS):
                            continue
                        if d.seq > waited[d.eng]:
                            e.wait_ge(sems[d.eng], d.seq)
                            waited[d.eng] = d.seq
                    for d in op.force:
                        if d.seq > waited[d.eng]:
                            e.wait_ge(sems[d.eng], d.seq)
                            waited[d.eng] = d.seq
                    ins = op.fn(e)
                    if op.inc:
                        ins.then_inc(sems[op.eng], 16 if op.eng in DMA_ENGS else 1)
            return body

        blk = nc.Block()
        with blk:
            for name in ("sync", "gpsimd", "tensor", "scalar", "vector"):
                if name in bodies:
                    getattr(blk, name)(make_body(bodies[name]))


class Buf:
    __slots__ = ("producer", "readers")

    def __init__(self):
        self.producer = None
        self.readers = []

    def wdeps(self):
        d = list(self.readers)
        if self.producer is not None:
            d.append(self.producer)
        return d

    def rdeps(self):
        return [self.producer] if self.producer is not None else []

    def wrote(self, op):
        self.producer = op
        self.readers = []

    def read(self, op):
        self.readers.append(op)



def build_program(debug=(), expose=()):
    nc = bass.Bass("TRN2", target_bir_lowering=False, detect_race_conditions=False)
    S = Sched()

    din = {}
    for nm, shp in [
        ("x_own", [TO, D]), ("xT_own", [D, TO]), ("xT_oth", [D, TO]),
        ("w1a", [D, H4]), ("b1a_c", [128, 32]), ("b1a_r", [1, H4]),
        ("w2a", [D, D]),
        ("gam_c", [128, 8]), ("bet_c", [128, 8]),
        ("b2a_r", [1, D]),
        ("w1f_T", [D, H4]), ("b1f_r", [1, H4]),
        ("w2f_T", [H4, D]), ("b2f_r", [1, D]),
        ("ident", [128, 128]),
    ]:
        din[nm] = nc.dram_tensor(nm, shp, F32, kind="ExternalInput").ap()

    y_out = nc.dram_tensor("y", [TO, D], F32, kind="ExternalOutput").ap()

    dbg = {}
    for nm in debug:
        shp, dt = DEBUG_SPECS[nm]
        dbg[nm] = nc.dram_tensor("dbg_" + nm, shp, dt, kind="ExternalOutput").ap()

    scr = {}
    kind = lambda nm: "ExternalOutput" if nm in expose else "Internal"
    scr["kT"] = nc.dram_tensor("kT_d", [D, FS], BF16, kind=kind("kT")).ap()
    scr["v"] = nc.dram_tensor("v_d", [FS, D], BF16, kind=kind("v")).ap()
    scr["qT"] = nc.dram_tensor("qT_d", [D, TO], BF16, kind=kind("qT")).ap()
    scr["uT"] = nc.dram_tensor("uT_d", [D, TO], F32, kind=kind("uT")).ap()
    scr["x1"] = nc.dram_tensor("x1_d", [TO, D], F32, kind=kind("x1")).ap()
    scr["xq"] = nc.dram_tensor("xq_d", [TO, D], BF16, kind=kind("xq")).ap()
    scr["gq"] = nc.dram_tensor("gq_d", [TO, H4], BF16, kind=kind("gq")).ap()
    scr["g"] = nc.dram_tensor("g_d", [TO, H4], F32, kind=kind("g")).ap()
    scr["w1q"] = nc.dram_tensor("w1q_d", [D, H4], BF16, kind=kind("w1q")).ap()
    scr["w2q"] = nc.dram_tensor("w2q_d", [H4, D], BF16, kind=kind("w2q")).ap()
    scr["row"] = nc.dram_tensor("row_d", [4, H4], F32, kind=kind("row")).ap()
    scr["col"] = nc.dram_tensor("col_d", [4, 128], F32, kind=kind("col")).ap()

    with ExitStack() as TOP:
        sems = {e: TOP.enter_context(nc.semaphore("sem" + e)) for e in ENGINES}

        sb = lambda *a, **k: TOP.enter_context(nc.sbuf_tensor(*a, **k))
        ones_bf = sb("ones_bf", [128, 1], BF16)
        identity = sb("identity", [128, 128], F32)
        b1a_c = sb("b1a_cs", [128, 32], F32)
        gam_c = sb("gam_cs", [128, 8], F32)
        bet_c = sb("bet_cs", [128, 8], F32)
        xs_all = sb("xs_all", [128, 16], F32)
        gs_all = sb("gs_all", [128, 16], F32)
        d1_all = sb("d1_all", [128, 16], F32)
        d2_all = sb("d2_all", [128, 16], F32)
        g127_all = sb("g127_all", [128, 16], F32)
        srow = sb("srow", [1, 8 * 512], F32)
        scol = sb("scol", [128, 24], F32)
        bn6 = sb("bn6", [128, 12], F32)
        bn2 = sb("bn2", [128, 2], F32)

        PS = TOP.enter_context(nc.psum_tensor("PS", [128, 4096], F32))
        psum = [Buf() for _ in range(8)]
        def bank(i, lo=0, width=512):
            return PS[lo:128, i * 512: i * 512 + width]

        def dma(eng, out_ap, in_ap, deps, tag="", slow=False):
            return S.add(eng, lambda e, o=out_ap, i=in_ap, s=slow: e.dma_start(
                out=o, in_=i, allow_slow_non_contiguous=s), deps, tag)

        gjunk = sb("gjunk", [128, 2], BF16)
        syjunk = sb("syjunk", [128, 2], F32)

        def gdma_gated(out_ap, in_ap, deps, tag=""):
            d1 = dma("G", out_ap, in_ap, deps, tag)
            d2 = dma("G", gjunk[:], din["ident"][:, 0:2], [d1], tag + "_gate")
            return d2

        def dma_t(out_ap, in_ap, deps, tag=""):
            d1 = S.add("SY", lambda e, o=out_ap, i=in_ap: e.dma_start_transpose(out=o, in_=i),
                       deps, tag)
            d2 = S.add("SY", lambda e: e.dma_start(out=syjunk[:], in_=din["ident"][:, 2:4]),
                       [d1], tag + "_gate")
            return d2

        def bcast_load(col_ap, dram_t, off, n, deps, tag="bc"):
            src = AP(tensor=dram_t.tensor, offset=off, ap=[[0, 128], [1, n]])
            return dma("SY", col_ap, src, deps, tag)

        ld_ident = dma("SY", identity[:], din["ident"], [])
        ld_b1ac = dma("SY", b1a_c[:], din["b1a_c"], [])
        ld_gam = dma("SY", gam_c[:], din["gam_c"], [])
        ld_bet = dma("SY", bet_c[:], din["bet_c"], [])
        op_ones = S.add("V", lambda e: e.memset(ones_bf[:], 1.0), [])
        eps_sb = sb("eps_sb", [128, 2], F32)
        h128_sb = sb("h128_sb", [128, 1], F32)
        op_h128 = S.add("V", lambda e: e.memset(h128_sb[:], 128.5), [])
        stjunk = sb("stjunk", [1, 2], F32)
        op_stj = S.add("V", lambda e: e.memset(stjunk[:], 0.0), [])
        op_eps0 = S.add("V", lambda e: e.memset(eps_sb[:, 0:1], 100000.0), [])
        op_eps = S.add("V", lambda e: e.reciprocal(eps_sb[:, 1:2], eps_sb[:, 0:1]),
                       [], force=[op_eps0])
        for wf in (ACT.Silu, ACT.Sqrt, ACT.Gelu, ACT.Square, ACT.Identity, ACT.Abs):
            S.add("A", lambda e, wf=wf: e.activation(
                out=bn6[:, 0:1], in_=eps_sb[:, 1:2], func=wf), [op_eps], "warm")

        with ExitStack() as S1:
            sb1 = lambda *a, **k: S1.enter_context(nc.sbuf_tensor(*a, **k))
            w1bf = sb1("w1bf", [128, 8 * H4], BF16)
            xtb = [sb1(f"xtb{i}", [128, 8 * 512], BF16) for i in range(2)]
            xtb_buf = [Buf(), Buf()]
            stg_f = [sb1(f"stgf{i}", [128, 512], F32) for i in range(2)]
            stg_b = [sb1(f"stgb{i}", [128, 512], BF16) for i in range(4)]
            stgv_f = [sb1(f"stgvf{i}", [128, 1024], F32) for i in range(2)]
            stgv_b = [sb1(f"stgvb{i}", [128, 1024], BF16) for i in range(2)]
            b1v_rep = sb1("b1v_rep", [128, D], F32)
            stg_f_buf = [Buf(), Buf()]
            stg_b_buf = [Buf() for _ in range(4)]
            stgv_f_buf = [Buf(), Buf()]
            stgv_b_buf = [Buf(), Buf()]

            w1_src = din["w1a"].rearrange("(kc p) c -> p kc c", p=128)
            ld_w1bf = gdma_gated(w1bf.ap().rearrange("p (kc c) -> p kc c", kc=8),
                                 w1_src, [], "w1bf")
            ld_b1v = bcast_load(b1v_rep[:], din["b1a_r"], D, D, [], "b1v")

            fcnt = [0]
            kqv_spill_ops = []

            stg_w = [sb1(f"stgw{i}", [128, H4], F32) for i in range(2)]
            stg_wj = sb1("stgwj", [128, H4], BF16)
            stg_w_buf = [Buf(), Buf()]
            stg_wj_buf = Buf()
            wacc_buf = [Buf(), Buf()]
            z1 = S.add("V", lambda e: e.memset(scol[:, 8:9], 0.0), [], "wacc0a")
            wacc_buf[0].wrote(z1)
            z2 = S.add("V", lambda e: e.memset(scol[:, 16:17], 0.0), [], "wacc0b")
            wacc_buf[1].wrote(z2)

            def wstat_load(i):
                if i < 8:
                    src = din["w1f_T"][i * 128:(i + 1) * 128, :]
                    d = dma("G", stg_w[i % 2][:], src, stg_w_buf[i % 2].wdeps(),
                            f"wstld{i}")
                else:
                    g = i - 8
                    src = AP(tensor=din["w2f_T"].tensor, offset=g * 4 * 128 * D,
                             ap=[[D, 128], [128 * D, 4], [1, D]])
                    d = dma("G", stg_w[i % 2].ap().rearrange("p (c o) -> p c o", c=4),
                            src, stg_w_buf[i % 2].wdeps(), f"wstld{i}")
                stg_w_buf[i % 2].wrote(d)
                return d

            wstat_lds = [wstat_load(0), wstat_load(1)]

            def wstat_step(i):
                w2 = i >= 8
                acol, scol_a = (16, 17) if w2 else (8, 9)
                ab = S.add("A", lambda e, a=scol_a: e.activation(
                    out=stg_wj[:], in_=stg_w[i % 2][:], func=ACT.Abs,
                    accum_out=scol[:, a: a + 1]),
                    [wstat_lds[i]] + stg_wj_buf.wdeps() + wacc_buf[w2].rdeps(),
                    f"wabs{i}")
                stg_w_buf[i % 2].read(ab)
                stg_wj_buf.wrote(ab)
                ac = S.add("V", lambda e, a=acol, b=scol_a: e.tensor_add(
                    scol[:, a: a + 1], scol[:, a: a + 1], scol[:, b: b + 1]),
                    [ab] + wacc_buf[w2].wdeps(), f"wacs{i}")
                wacc_buf[w2].wrote(ac)
                stg_wj_buf.read(ac)
                if i + 2 < 16:
                    wstat_lds.append(wstat_load(i + 2))

            wmm = S.add("T", lambda e: e.matmul(
                bank(0)[0:128, 0:128], identity[:], identity[:],
                start=True, stop=True), [ld_ident] + psum[0].wdeps(), "warmmm")
            psum[0].wrote(wmm)
            wev = S.add("A", lambda e: e.activation(
                out=bn6[:, 1:2], in_=bank(0)[0:128, 0:1], func=ACT.Identity),
                psum[0].rdeps(), "warmev")
            psum[0].read(wev)

            def p1_block(tb):
                src = din["xT_own"] if tb < 4 else din["xT_oth"]
                t0 = (tb % 4) * 512
                xa = xtb[tb % 2]
                xbuf = xtb_buf[tb % 2]
                sap = AP(tensor=src.tensor, offset=t0,
                         ap=[[TO, 128], [128 * TO, 8], [1, 512]])
                ld = gdma_gated(xa.ap().rearrange("p (kc t) -> p kc t", kc=8), sap,
                                xbuf.wdeps(), f"xtb{tb}")
                xbuf.wrote(ld)

                ocs = []
                if tb < 4:
                    ocs += [("u", oc) for oc in range(8)]
                    ocs += [("q", oc) for oc in range(8)]
                ocs += [("k", oc) for oc in range(8)]
                for kind, oc in ocs:
                    base_col = {"u": 0, "q": 2048, "k": 3072}[kind] + oc * 128
                    bname = fcnt[0] % 4
                    bb = psum[bname]
                    mms = []
                    for kc in range(8):
                        deps = xbuf.rdeps() + [ld_w1bf] + (bb.wdeps() if kc == 0 else [])
                        mm = S.add("T", lambda e, kc=kc, bn=bname, bc=base_col, xa=xa: e.matmul(
                            bank(bn), w1bf[:, kc * H4 + bc: kc * H4 + bc + 128],
                            xa[:, kc * 512: (kc + 1) * 512],
                            start=(kc == 0), stop=(kc == 7)),
                            deps, f"p1mm{kind}{oc}")
                        mms.append(mm)
                        xbuf.read(mm)
                    bb.wrote(mms[-1])
                    cchunk = {"u": oc, "q": 16 + oc, "k": 24 + oc}[kind]
                    if kind == "u":
                        st = stg_f[fcnt[0] % 2]
                        stb = stg_f_buf[fcnt[0] % 2]
                        odt = st
                    else:
                        st = stg_b[fcnt[0] % 4]
                        stb = stg_b_buf[fcnt[0] % 4]
                        odt = st
                    ev = S.add("A", lambda e, bn=bname, st=odt, cc=cchunk: e.activation(
                        out=st[:], in_=bank(bn), func=ACT.Silu,
                        bias=b1a_c[:, cc: cc + 1], scale=1.0),
                        bb.rdeps() + stb.wdeps() + [ld_b1ac], f"p1ev{kind}{oc}")
                    bb.read(ev)
                    stb.wrote(ev)
                    if kind == "u":
                        dst = scr["uT"][oc * 128:(oc + 1) * 128, t0: t0 + 512]
                    elif kind == "q":
                        dst = scr["qT"][oc * 128:(oc + 1) * 128, t0: t0 + 512]
                    else:
                        dst = scr["kT"][oc * 128:(oc + 1) * 128, tb * 512:(tb + 1) * 512]
                    sp = dma("ST", dst, odt[:], stb.rdeps(), f"p1sp{kind}{oc}")
                    stb.read(sp)
                    kqv_spill_ops.append(sp)
                    fcnt[0] += 1

                for tl in range(4):
                    bsel = 4 + (tl % 2) * 2
                    bb0, bb1 = psum[bsel], psum[bsel + 1]
                    mms = []
                    for kc in range(8):
                        for nt in range(2):
                            bb = (bb0, bb1)[nt]
                            deps = xbuf.rdeps() + [ld_w1bf] + (bb.wdeps() if kc == 0 else [])
                            mm = S.add("T", lambda e, kc=kc, nt=nt, tl=tl, bs=bsel, xa=xa: e.matmul(
                                bank(bs + nt),
                                xa[:, kc * 512 + tl * 128: kc * 512 + tl * 128 + 128],
                                w1bf[:, kc * H4 + 1024 + nt * 512: kc * H4 + 1024 + (nt + 1) * 512],
                                start=(kc == 0), stop=(kc == 7)),
                                deps, "p1vmm")
                            mms.append(mm)
                            xbuf.read(mm)
                            if kc == 0:
                                bb.wrote(mm)
                            else:
                                bb.producer = mm
                    vf = stgv_f[tl % 2]
                    vfb = stgv_f_buf[tl % 2]
                    addv = S.add("V", lambda e, bs=bsel, vf=vf: e.tensor_add(
                        vf[:], PS[0:128, bs * 512:(bs + 2) * 512], b1v_rep[:]),
                        bb0.rdeps() + bb1.rdeps() + vfb.wdeps() + [ld_b1v], "p1vadd")
                    bb0.read(addv); bb1.read(addv); vfb.wrote(addv)
                    vb = stgv_b[tl % 2]
                    vbb = stgv_b_buf[tl % 2]
                    sil = S.add("A", lambda e, vf=vf, vb=vb: e.activation(
                        out=vb[:], in_=vf[:], func=ACT.Silu),
                        vfb.rdeps() + vbb.wdeps(), "p1vsil")
                    vfb.read(sil); vbb.wrote(sil)
                    sp = dma("ST", scr["v"][tb * 512 + tl * 128: tb * 512 + (tl + 1) * 128, :],
                             vb[:], vbb.rdeps(), "p1vsp")
                    vbb.read(sp)
                    kqv_spill_ops.append(sp)

            for tb in range(8):
                p1_block(tb)
                wstat_step(tb * 2)
                wstat_step(tb * 2 + 1)

            spc1 = dma("ST", scr["col"][0:1, :], scol[:, 8:9], wacc_buf[0].rdeps(), "accsp1")
            wacc_buf[0].read(spc1)
            ldr1 = dma("SY", srow[0:1, 0:128], scr["col"][0:1, :], [spc1], "accld1")
            rs1 = S.add("V", lambda e: e.reduce_sum(srow[0:1, 128:129], srow[0:1, 0:128], AX),
                        [ldr1], "wsum1")
            ws1o = S.add("V", lambda e: e.tensor_scalar(
                out=srow[0:1, 129:130], in0=srow[0:1, 128:129],
                scalar1=1.0 / (D * H4), scalar2=QEPS, op0=ALU.mult, op1=ALU.max),
                [], "ws1", force=[rs1])
            th1 = S.add("V", lambda e: e.tensor_scalar_mul(srow[0:1, 130:131], srow[0:1, 129:130], 0.5),
                        [], "thr1", force=[ws1o])
            th1n = S.add("V", lambda e: e.tensor_scalar_mul(srow[0:1, 131:132], srow[0:1, 130:131], -1.0),
                         [], "thr1n", force=[th1])
            d1f = S.add("V", lambda e: e.tensor_scalar_mul(srow[0:1, 132:133], srow[0:1, 129:130], 1.0 / 127.0),
                        [], "d1f", force=[ws1o])
            spr1 = dma("ST", scr["row"][3:4, 0:5], srow[0:1, 128:133], [th1n, d1f], "rowsp1")
            ld_th1 = bcast_load(scol[:, 10:11], scr["row"], 3 * H4 + 2, 1, [spr1], "th1c")
            ld_th1n = bcast_load(scol[:, 11:12], scr["row"], 3 * H4 + 3, 1, [spr1], "th1nc")
            ld_d1f = bcast_load(scol[:, 12:13], scr["row"], 3 * H4 + 4, 1, [spr1], "d1fc")
            spc2 = dma("ST", scr["col"][1:2, :], scol[:, 16:17], wacc_buf[1].rdeps(), "accsp2")
            wacc_buf[1].read(spc2)
            ldr2 = dma("SY", srow[0:1, 256:384], scr["col"][1:2, :], [spc2], "accld2")
            rs2 = S.add("V", lambda e: e.reduce_sum(srow[0:1, 200:201], srow[0:1, 256:384], AX),
                        [ldr2], "wsum2")
            ws2o = S.add("V", lambda e: e.tensor_scalar(
                out=srow[0:1, 201:202], in0=srow[0:1, 200:201],
                scalar1=1.0 / (D * H4), scalar2=QEPS, op0=ALU.mult, op1=ALU.max),
                [], "ws2", force=[rs2])
            th2 = S.add("V", lambda e: e.tensor_scalar_mul(srow[0:1, 202:203], srow[0:1, 201:202], 0.5),
                        [], "thr2", force=[ws2o])
            th2n = S.add("V", lambda e: e.tensor_scalar_mul(srow[0:1, 203:204], srow[0:1, 202:203], -1.0),
                         [], "thr2n", force=[th2])
            d2f = S.add("V", lambda e: e.tensor_scalar_mul(srow[0:1, 204:205], srow[0:1, 201:202], 1.0 / 127.0),
                        [], "d2f", force=[ws2o])
            spr2 = dma("ST", scr["row"][3:4, 8:13], srow[0:1, 200:205], [th2n, d2f], "rowsp2")
            ld_th2 = bcast_load(scol[:, 18:19], scr["row"], 3 * H4 + 10, 1, [spr2], "th2c")
            ld_th2n = bcast_load(scol[:, 19:20], scr["row"], 3 * H4 + 11, 1, [spr2], "th2nc")
            ld_d2f = bcast_load(scol[:, 20:21], scr["row"], 3 * H4 + 12, 1, [spr2], "d2fc")

        S.fence()
        with ExitStack() as S2:
            sb2 = lambda *a, **k: S2.enter_context(nc.sbuf_tensor(*a, **k))
            z_all = sb2("z_all", [128, 8 * TO], F32)
            z_buf = [Buf() for _ in range(8)]

            with ExitStack() as S2a:
                sb2a = lambda *a, **k: S2a.enter_context(nc.sbuf_tensor(*a, **k))
                NB = 2
                k_sb = [sb2a(f"k_sb{i}", [128, FS], BF16) for i in range(NB)]
                qm_he = [sb2a(f"qm_he{i}", [128, TO], BF16) for i in range(NB)]
                qm_ho = [sb2a(f"qm_ho{i}", [128, TO], BF16) for i in range(NB)]
                u_sb = [sb2a(f"u_sb{i}", [128, TO], F32) for i in range(NB)]
                vm_sb = [sb2a(f"vm_sb{i}", [128, 2 * FS], BF16) for i in range(NB)]
                sc_sb = [sb2a(f"sc_sb{i}", [128, 1024], BF16) for i in range(2)]
                ld_bufs = {nm: [Buf() for _ in range(NB)]
                           for nm in ("k_sb", "qm_he", "qm_ho", "u_sb", "vm_sb")}
                sc_bufs = [Buf(), Buf()]

                for i in range(NB):
                    zm = S.add("V", lambda e, i=i: e.memset(vm_sb[i][:], 0.0), [], "vmzero")
                    ld_bufs["vm_sb"][i].wrote(zm)
                    z1 = S.add("V", lambda e, i=i: e.memset(qm_he[i][64:128, :], 0.0), [], "qmz1")
                    ld_bufs["qm_he"][i].wrote(z1)
                    z2 = S.add("V", lambda e, i=i: e.memset(qm_ho[i][0:64, :], 0.0), [], "qmz2")
                    ld_bufs["qm_ho"][i].wrote(z2)

                for hp in range(8):
                    sl = hp % NB
                    loads = {}
                    loads["k_sb"] = dma("SY", k_sb[sl][:], scr["kT"][hp * 128:(hp + 1) * 128, :],
                                        ld_bufs["k_sb"][sl].wdeps() + (kqv_spill_ops if hp == 0 else []),
                                        f"k{hp}")
                    loads["qm_he"] = dma("SY", qm_he[sl][0:64, :], scr["qT"][hp * 128: hp * 128 + 64, :],
                                         ld_bufs["qm_he"][sl].wdeps(), f"q_he{hp}")
                    loads["qm_ho"] = dma("SY", qm_ho[sl][64:128, :], scr["qT"][hp * 128 + 64: hp * 128 + 128, :],
                                         ld_bufs["qm_ho"][sl].wdeps(), f"q_ho{hp}")
                    loads["u_sb"] = dma("SY", u_sb[sl][:], scr["uT"][hp * 128:(hp + 1) * 128, :],
                                        ld_bufs["u_sb"][sl].wdeps(), f"u{hp}")
                    vdeps = ld_bufs["vm_sb"][sl].wdeps()
                    src_he = AP(tensor=scr["v"].tensor, offset=hp * 128,
                                ap=[[D, 128], [128 * D, 32], [1, 64]])
                    dst_he = AP(tensor=vm_sb[sl], offset=0,
                                ap=[[2 * FS, 128], [256, 32], [1, 64]])
                    l1 = dma("SY", dst_he, src_he, vdeps, f"vmhe{hp}")
                    src_ho = AP(tensor=scr["v"].tensor, offset=hp * 128 + 64,
                                ap=[[D, 128], [128 * D, 32], [1, 64]])
                    dst_ho = AP(tensor=vm_sb[sl], offset=192,
                                ap=[[2 * FS, 128], [256, 32], [1, 64]])
                    l2 = dma("SY", dst_ho, src_ho, vdeps, f"vmho{hp}")
                    ld_bufs["vm_sb"][sl].producer = l2
                    ld_bufs["vm_sb"][sl].readers.append(l1)
                    for nm in ("k_sb", "qm_he", "qm_ho", "u_sb"):
                        ld_bufs[nm][sl].wrote(loads[nm])
                    vm_lds = [l1, l2]
                    kq = {"k": k_sb[sl], "he": qm_he[sl], "ho": qm_ho[sl]}

                    for qt in range(4):
                        q0 = qt * 512
                        avb = psum[4]
                        sils = [None] * 32

                        def emit_sc(kt):
                            par = kt % 2
                            smm = []
                            for half, b in (("he", 0), ("ho", 1)):
                                bb = psum[par * 2 + b]
                                mm = S.add("T", lambda e, kt=kt, par=par, b=b, half=half, kq=kq, q0=q0: e.matmul(
                                    bank(par * 2 + b),
                                    kq["k"][:, kt * 128:(kt + 1) * 128],
                                    kq[half][:, q0: q0 + 512], start=True, stop=True),
                                    ld_bufs["k_sb"][sl].rdeps()
                                    + ld_bufs["qm_" + half][sl].rdeps() + bb.wdeps(),
                                    f"sc{half}")
                                ld_bufs["k_sb"][sl].read(mm)
                                ld_bufs["qm_" + half][sl].read(mm)
                                bb.wrote(mm)
                                smm.append(mm)
                            sil = S.add("A", lambda e, par=par: e.activation(
                                out=sc_sb[par][:], in_=PS[0:128, par * 1024:(par + 1) * 1024],
                                func=ACT.Silu),
                                smm + sc_bufs[par].wdeps(), "sil")
                            psum[par * 2].read(sil)
                            psum[par * 2 + 1].read(sil)
                            sc_bufs[par].wrote(sil)
                            sils[kt] = sil

                        def emit_av(kt):
                            par = kt % 2
                            for half, moff, off in (("he", 0, 0), ("ho", 128, 512)):
                                first = (kt == 0 and half == "he")
                                last = (kt == 31 and half == "ho")
                                deps = [sils[kt]] + vm_lds
                                if first:
                                    deps = deps + avb.wdeps()
                                mm = S.add("T", lambda e, kt=kt, par=par, moff=moff, off=off, f=first, l=last, sl=sl: e.matmul(
                                    bank(4), vm_sb[sl][:, kt * 256 + moff: kt * 256 + moff + 128],
                                    sc_sb[par][:, off: off + 512],
                                    start=f, stop=l), deps, f"av{half}")
                                sc_bufs[par].read(mm)
                                ld_bufs["vm_sb"][sl].read(mm)
                                if first:
                                    avb.wrote(mm)
                                else:
                                    avb.producer = mm

                        emit_sc(0)
                        for kt in range(1, 32):
                            emit_sc(kt)
                            emit_av(kt - 1)
                        emit_av(31)

                        st = S.add("V", lambda e, hp=hp, q0=q0, sl=sl: e.scalar_tensor_tensor(
                            out=z_all[:, hp * TO + q0: hp * TO + q0 + 512],
                            in0=bank(4), scalar=float(2.0 ** -12),
                            in1=u_sb[sl][:, q0: q0 + 512],
                            op0=ALU.mult, op1=ALU.mult),
                            avb.rdeps() + ld_bufs["u_sb"][sl].rdeps()
                            + (z_buf[hp].wdeps() if qt == 0 else []), "avu")
                        avb.read(st)
                        ld_bufs["u_sb"][sl].read(st)
                        if qt == 0:
                            z_buf[hp].wrote(st)
                        else:
                            z_buf[hp].producer = st

            if "z" in dbg:
                for slot in range(8):
                    d = dma("ST", dbg["z"][slot * 128:(slot + 1) * 128, :],
                            z_all[:, slot * TO:(slot + 1) * TO],
                            z_buf[slot].rdeps(), "dbgz")
                    z_buf[slot].read(d)

            S.fence()
            with ExitStack() as S3:
                sb3 = lambda *a, **k: S3.enter_context(nc.sbuf_tensor(*a, **k))
                w2g = sb3("w2g", [128, 8 * D], BF16)
                zbf = sb3("zbf", [128, 8 * 512], BF16)
                zsq = sb3("zsq", [128, 8 * 512], BF16)
                P_sb = sb3("P_sb", [128, 8 * 512], F32)
                xt_sb = [sb3(f"xt_sb{i}", [128, D], F32) for i in range(2)]
                t0_sb = sb3("t0_sb", [128, D], F32)
                t1_sb = sb3("t1_sb", [128, D], F32)
                r_sb = sb3("r_sb", [128, D], F32)
                x1_sb = sb3("x1_sb", [128, D], F32)
                lt_sb = sb3("lt_sb", [128, D], F32)
                lr_sb = sb3("lr_sb", [128, D], F32)
                lu_sb = sb3("lu_sb", [128, D], F32)
                xq_sb = sb3("xq_sb", [128, D], BF16)
                stc_sb = sb3("stc_sb", [128, 8], F32)
                tmpbf = sb3("tmpbf", [128, D], BF16)
                bet_bf = sb3("bet_bf", [128, 8], BF16)
                c1_rep = sb3("c1_rep", [128, D], F32)
                c2_rep = sb3("c2_rep", [128, D], F32)
                crow = srow
                b2a_rs = sb3("b2a_rs", [1, D], F32)

                ld_b2ar = dma("SY", b2a_rs[:], din["b2a_r"], [])

                with ExitStack() as S3p:
                    sb3p = lambda *a, **k: S3p.enter_context(nc.sbuf_tensor(*a, **k))
                    w2st = sb3p("w2st", [128, 4 * D], F32)
                    w2st_buf = Buf()

                    cb1 = S.add("V", lambda e: e.tensor_copy(out=bet_bf[:], in_=bet_c[:]),
                                [ld_bet], "betbf")
                    w2g_ops = [None] * 8
                    tmp_buf = Buf()
                    c2b0, c2b1 = psum[6], psum[7]
                    mmc2 = []
                    for grp in range(2):
                        src_w2 = AP(tensor=din["w2a"].tensor, offset=grp * 4 * 128 * D,
                                    ap=[[D, 128], [128 * D, 4], [1, D]])
                        ld_w2 = dma("SY", w2st.ap().rearrange("p (c o) -> p c o", c=4),
                                    src_w2, w2st_buf.wdeps(), "w2ld")
                        w2st_buf.wrote(ld_w2)
                        for ci in range(4):
                            c = grp * 4 + ci
                            o = S.add("V", lambda e, c=c, ci=ci: e.tensor_scalar(
                                out=w2g[:, c * D:(c + 1) * D], in0=w2st[:, ci * D:(ci + 1) * D],
                                scalar1=gam_c[:, c: c + 1], scalar2=None, op0=ALU.mult),
                                [ld_w2, ld_gam], f"w2g{c}")
                            w2st_buf.read(o)
                            w2g_ops[c] = o
                            cast = S.add("V", lambda e, ci=ci: e.tensor_copy(
                                out=tmpbf[:], in_=w2st[:, ci * D:(ci + 1) * D]),
                                [ld_w2] + tmp_buf.wdeps(), "c2cast")
                            w2st_buf.read(cast)
                            tmp_buf.wrote(cast)
                            for nt in range(2):
                                first, last = c == 0, c == 7
                                bb = (c2b0, c2b1)[nt]
                                deps = tmp_buf.rdeps() + [cb1]
                                if first:
                                    deps += bb.wdeps()
                                mm = S.add("T", lambda e, nt=nt, c=c, f=first, l=last: e.matmul(
                                    bank(6 + nt)[0:1, :], bet_bf[:, c: c + 1],
                                    tmpbf[:, nt * 512:(nt + 1) * 512],
                                    start=f, stop=l), deps, "c2mm")
                                tmp_buf.read(mm)
                                if first:
                                    bb.wrote(mm)
                                else:
                                    bb.producer = mm
                                mmc2.append(mm)
                    evc2 = S.add("V", lambda e: e.tensor_add(
                        crow[0:1, D: 2 * D], PS[0:1, 6 * 512: 8 * 512], b2a_rs[:]),
                        [mmc2[-1], ld_b2ar], "c2ev")
                    c2b0.read(evc2); c2b1.read(evc2)

                    c1b0, c1b1 = psum[6], psum[7]
                    mmc1 = []
                    for c in range(8):
                        for nt in range(2):
                            first, last = c == 0, c == 7
                            bb = (c1b0, c1b1)[nt]
                            deps = [w2g_ops[c], op_ones]
                            if first:
                                deps += bb.wdeps() + [evc2]
                            mm = S.add("T", lambda e, nt=nt, c=c, f=first, l=last: e.matmul(
                                bank(6 + nt)[0:1, :], ones_bf[:, 0:1],
                                w2g[:, c * D + nt * 512: c * D + (nt + 1) * 512],
                                start=f, stop=l), deps, "c1mm")
                            if first:
                                bb.wrote(mm)
                            else:
                                bb.producer = mm
                            mmc1.append(mm)
                    evc1 = S.add("V", lambda e: e.tensor_copy(
                        out=crow[0:1, 0:D], in_=PS[0:1, 6 * 512: 8 * 512]),
                        [mmc1[-1]], "c1ev")
                    c1b0.read(evc1); c1b1.read(evc1)

                    spc = dma("ST", scr["row"][2:3, 0: 2 * D], crow[0:1, 0: 2 * D], [evc1, evc2], "cspill")
                    ld_c1 = bcast_load(c1_rep[:], scr["row"], 2 * H4, D, [spc], "c1rep")
                    ld_c2 = bcast_load(c2_rep[:], scr["row"], 2 * H4 + D, D, [spc], "c2rep")

                zbf_buf = Buf()
                zsq_buf = Buf()
                P_buf = Buf()
                stc_buf = Buf()
                row_bufs = [Buf() for _ in range(8)]
                xt_buf = [Buf(), Buf()]
                misc = {nm: Buf() for nm in ("t0", "t1", "r", "x1", "lt", "lr", "lu",
                                             "xq", "bn6", "bn2", "sq")}

                x1_spills = []
                xq_spills = []
                w1q_spill_ops = [[] for _ in range(8)]
                w2q_spill_ops = [[] for _ in range(32)]

                for qt in range(4):
                    q0 = qt * 512
                    casts = []
                    sqs = []
                    for slot in range(8):
                        cp = S.add("V", lambda e, slot=slot, q0=q0: e.tensor_copy(
                            out=zbf[:, slot * 512:(slot + 1) * 512],
                            in_=z_all[:, slot * TO + q0: slot * TO + q0 + 512]),
                            z_buf[slot].rdeps() + (zbf_buf.wdeps() if slot == 0 else []),
                            "zcast")
                        z_buf[slot].read(cp)
                        casts.append(cp)
                        sq = S.add("A", lambda e, slot=slot, q0=q0: e.activation(
                            out=zsq[:, slot * 512:(slot + 1) * 512],
                            in_=z_all[:, slot * TO + q0: slot * TO + q0 + 512],
                            func=ACT.Square),
                            z_buf[slot].rdeps() + (zsq_buf.wdeps() if slot == 0 else []),
                            "zsq")
                        z_buf[slot].read(sq)
                        sqs.append(sq)
                    zbf_buf.wrote(casts[-1])
                    zsq_buf.wrote(sqs[-1])

                    smz, smq = psum[6], psum[7]
                    mz = mq = None
                    for slot in range(8):
                        first, last = slot == 0, slot == 7
                        deps = [casts[slot], op_ones] + (smz.wdeps() if first else [])
                        mz = S.add("T", lambda e, slot=slot, f=first, l=last: e.matmul(
                            bank(6)[0:1, :], ones_bf[:, 0:1],
                            zbf[:, slot * 512:(slot + 1) * 512], start=f, stop=l),
                            deps, "szmm")
                        zbf_buf.read(mz)
                        if first:
                            smz.wrote(mz)
                        else:
                            smz.producer = mz
                        deps = [sqs[slot], op_ones] + (smq.wdeps() if first else [])
                        mq = S.add("T", lambda e, slot=slot, f=first, l=last: e.matmul(
                            bank(7)[0:1, :], ones_bf[:, 0:1],
                            zsq[:, slot * 512:(slot + 1) * 512], start=f, stop=l),
                            deps, "sqmm")
                        zsq_buf.read(mq)
                        if first:
                            smq.wrote(mq)
                        else:
                            smq.producer = mq

                    def rsl(i):
                        return srow[0:1, i * 512:(i + 1) * 512]
                    o1 = S.add("V", lambda e: e.tensor_scalar(
                        out=rsl(0), in0=bank(6)[0:1, :], scalar1=-1.0 / 1024.0,
                        scalar2=None, op0=ALU.mult),
                        [mz, ld_c1, ld_c2] + row_bufs[0].wdeps()
                        + ([spr1, spr2] if qt == 0 else []), "mu")
                    smz.read(o1); row_bufs[0].wrote(o1)
                    o2 = S.add("V", lambda e: e.tensor_scalar(
                        out=rsl(1), in0=bank(7)[0:1, :], scalar1=1.0 / 1024.0,
                        scalar2=None, op0=ALU.mult), [mq] + row_bufs[1].wdeps(), "msq")
                    smq.read(o2); row_bufs[1].wrote(o2)
                    o3 = S.add("V", lambda e: e.tensor_mul(rsl(2), rsl(0), rsl(0)),
                               [o1] + row_bufs[2].wdeps(), "musq")
                    row_bufs[2].wrote(o3)
                    o4 = S.add("V", lambda e: e.tensor_sub(rsl(3), rsl(1), rsl(2)),
                               [o2, o3] + row_bufs[3].wdeps(), "var")
                    row_bufs[3].wrote(o4)
                    o5 = S.add("A", lambda e: e.activation(
                        out=rsl(4), in_=rsl(3), func=ACT.Sqrt, bias=eps_sb[0:1, 1:2]),
                        [o4, op_eps] + row_bufs[4].wdeps(), "sqr")
                    row_bufs[4].wrote(o5)
                    o6 = S.add("V", lambda e: e.reciprocal(rsl(5), rsl(4)),
                               [o5] + row_bufs[5].wdeps(), "rstd")
                    row_bufs[5].wrote(o6)
                    o7 = S.add("V", lambda e: e.tensor_mul(rsl(6), rsl(0), rsl(5)),
                               [o6, o1] + row_bufs[6].wdeps(), "mr")
                    row_bufs[6].wrote(o7)

                    pev = []
                    for oc in range(8):
                        pb = psum[oc % 2]
                        mml = None
                        for c in range(8):
                            first, last = c == 0, c == 7
                            deps = [casts[c], w2g_ops[c]]
                            if first:
                                deps = deps + pb.wdeps()
                            mml = S.add("T", lambda e, oc=oc, c=c, f=first, l=last, b=oc % 2: e.matmul(
                                bank(b), w2g[:, c * D + oc * 128: c * D + oc * 128 + 128],
                                zbf[:, c * 512:(c + 1) * 512],
                                start=f, stop=l), deps, "pmm")
                            zbf_buf.read(mml)
                            if first:
                                pb.wrote(mml)
                            else:
                                pb.producer = mml
                        ev = S.add("V", lambda e, oc=oc, b=oc % 2: e.tensor_copy(
                            out=P_sb[:, oc * 512:(oc + 1) * 512], in_=bank(b)),
                            pb.rdeps() + P_buf.wdeps(), "pev")
                        pb.read(ev)
                        pev.append(ev)
                    P_buf.wrote(pev[-1])
                    if "P" in dbg:
                        d = dma("ST", dbg["P"][qt * 128:(qt + 1) * 128, :], P_sb[:],
                                [pev[-1]], "dbgP")
                        P_buf.read(d)

                    stb = psum[5]
                    trs = []
                    for tt in range(4):
                        for ri, rop in ((0, o6), (1, o7)):
                            tr = S.add("T", lambda e, tt=tt, ri=ri: e.transpose(
                                bank(5)[:, tt * 2 + ri: tt * 2 + ri + 1],
                                srow[0:1, (5 + ri) * 512 + tt * 128: (5 + ri) * 512 + (tt + 1) * 128],
                                identity[0:1, 0:1]),
                                [rop, ld_ident] + (stb.wdeps() if (tt == 0 and ri == 0) else []),
                                "sttr")
                            if tt == 0 and ri == 0:
                                stb.wrote(tr)
                            else:
                                stb.producer = tr
                            trs.append(tr)
                    evst = S.add("V", lambda e: e.tensor_copy(
                        out=stc_sb[:], in_=bank(5, 0, 8)),
                        [trs[-1]] + stc_buf.wdeps(), "stev")
                    stb.read(evst)
                    stc_buf.wrote(evst)
                    row_bufs[5].read(trs[-1])
                    row_bufs[6].read(trs[-1])

                    for tt in range(4):
                        yb0, yb1 = psum[2], psum[3]
                        ytr = []
                        for oc in range(8):
                            first = oc == 0
                            deps = [pev[oc], ld_ident]
                            if first:
                                deps = deps + yb0.wdeps() + yb1.wdeps()
                            tr = S.add("T", lambda e, oc=oc, tt=tt: e.transpose(
                                PS[0:128, 2 * 512 + oc * 128: 2 * 512 + (oc + 1) * 128],
                                P_sb[:, oc * 512 + tt * 128: oc * 512 + (tt + 1) * 128],
                                identity[:]), deps, "ytr")
                            P_buf.read(tr)
                            ytr.append(tr)
                        yb0.wrote(ytr[-1]); yb1.producer = ytr[-1]

                        xt = xt_sb[tt % 2]
                        xtb_ = xt_buf[tt % 2]
                        ldx = dma("SY", xt[:], din["x_own"][q0 + tt * 128: q0 + (tt + 1) * 128, :],
                                  xtb_.wdeps(), "ldx")
                        xtb_.wrote(ldx)

                        v1 = S.add("V", lambda e, xt=xt: e.tensor_add(t0_sb[:], xt[:], c2_rep[:]),
                                   [ldx, ld_c2] + misc["t0"].wdeps(), "t0")
                        xtb_.read(v1); misc["t0"].wrote(v1)
                        v2 = S.add("V", lambda e, tt=tt: e.scalar_tensor_tensor(
                            out=t1_sb[:], in0=c1_rep[:], scalar=stc_sb[:, tt * 2 + 1: tt * 2 + 2],
                            in1=t0_sb[:], op0=ALU.mult, op1=ALU.add),
                            [v1, ld_c1, evst] + misc["t1"].wdeps(), "t1")
                        stc_buf.read(v2); misc["t0"].read(v2); misc["t1"].wrote(v2)
                        v3 = S.add("V", lambda e, tt=tt: e.scalar_tensor_tensor(
                            out=r_sb[:], in0=PS[0:128, 2 * 512: 4 * 512],
                            scalar=stc_sb[:, tt * 2: tt * 2 + 1], in1=t1_sb[:],
                            op0=ALU.mult, op1=ALU.add),
                            [ytr[-1], v2] + misc["r"].wdeps(), "rtile")
                        yb0.read(v3); yb1.read(v3); stc_buf.read(v3)
                        misc["t1"].read(v3); misc["r"].wrote(v3)

                        s1 = S.add("A", lambda e: e.activation(
                            out=xq_sb[:], in_=r_sb[:], func=ACT.Identity,
                            accum_out=scol[:, 6:7]),
                            [v3] + misc["xq"].wdeps() + misc["bn6"].wdeps(), "ln2s")
                        s2 = S.add("A", lambda e: e.activation(
                            out=xq_sb[:], in_=r_sb[:], func=ACT.Square,
                            accum_out=scol[:, 7:8]),
                            [v3], "ln2q")
                        misc["bn6"].wrote(s2)
                        m1 = S.add("V", lambda e: e.tensor_scalar_mul(
                            scol[:, 2:3], scol[:, 6:7], 1.0 / 1024.0),
                            [s1, s2] + misc["bn2"].wdeps(), "ln2m1")
                        m2 = S.add("V", lambda e: e.tensor_mul(
                            scol[:, 3:4], scol[:, 2:3], scol[:, 2:3]), [], "ln2m2", force=[m1])
                        m3 = S.add("V", lambda e: e.tensor_scalar_mul(
                            scol[:, 4:5], scol[:, 7:8], 1.0 / 1024.0), [s2], "ln2m3")
                        m4 = S.add("V", lambda e: e.tensor_sub(
                            scol[:, 5:6], scol[:, 4:5], scol[:, 3:4]), [], "ln2m4", force=[m2, m3])
                        misc["bn2"].wrote(m4)
                        a1 = S.add("A", lambda e: e.activation(
                            out=scol[:, 0:1], in_=scol[:, 5:6], func=ACT.Sqrt, bias=eps_sb[:, 1:2]),
                            [m4, op_eps] + misc["sq"].wdeps(), "sq2")
                        misc["bn2"].read(a1); misc["sq"].wrote(a1)
                        r1 = S.add("V", lambda e: e.reciprocal(scol[:, 1:2], scol[:, 0:1]),
                                   [a1], "rc2")
                        misc["sq"].read(r1)
                        x1o = S.add("V", lambda e: e.tensor_scalar(
                            out=x1_sb[:], in0=r_sb[:], scalar1=scol[:, 2:3],
                            scalar2=scol[:, 1:2], op0=ALU.subtract, op1=ALU.mult),
                            [m4] + misc["x1"].wdeps(), "x1", force=[r1])
                        misc["r"].read(x1o); misc["x1"].wrote(x1o)
                        misc["xq"].read(x1o)

                        q1 = S.add("V", lambda e: e.tensor_reduce(
                            out=scol[:, 2:3], in_=x1_sb[:], axis=AX, op=ALU.max,
                            apply_absolute_value=True), [x1o], "xsr")
                        q2 = S.add("V", lambda e: e.tensor_scalar_max(
                            scol[:, 3:4], scol[:, 2:3], QEPS), [], "xsc", force=[q1])
                        q4 = S.add("V", lambda e: e.reciprocal(scol[:, 4:5], scol[:, 3:4]),
                                   [], "xsi", force=[q2])
                        q5 = S.add("V", lambda e: e.tensor_scalar_mul(
                            scol[:, 5:6], scol[:, 4:5], 127.0), [], "s127", force=[q4])
                        q3 = S.add("V", lambda e, qt=qt, tt=tt: e.tensor_copy(
                            out=xs_all[:, qt * 4 + tt: qt * 4 + tt + 1], in_=scol[:, 3:4]),
                            [], "xssave", force=[q2])

                        l1 = S.add("V", lambda e: e.tensor_scalar(
                            out=lt_sb[:], in0=x1_sb[:], scalar1=scol[:, 5:6],
                            scalar2=128.5, op0=ALU.mult, op1=ALU.add),
                            [q3, x1o] + misc["lt"].wdeps(), "lad0", force=[q5])
                        misc["x1"].read(l1); misc["lt"].wrote(l1)
                        lc = S.add("V", lambda e: e.tensor_copy(out=lr_sb[:], in_=lt_sb[:]),
                                   [l1] + misc["lr"].wdeps(), "ladc")
                        misc["lr"].wrote(lc)
                        prev = lc
                        for bbit in (128.0, 64.0, 32.0, 16.0, 8.0, 4.0, 2.0, 1.0):
                            pu = S.add("V", lambda e, bb=bbit: e.tensor_scalar(
                                out=lu_sb[:], in0=lr_sb[:], scalar1=bb, scalar2=bb,
                                op0=ALU.is_ge, op1=ALU.mult), [prev], "ladg")
                            prev = S.add("V", lambda e: e.tensor_sub(lr_sb[:], lr_sb[:], lu_sb[:]),
                                         [pu], "lads")
                        xqo = S.add("V", lambda e: e.scalar_tensor_tensor(
                            out=xq_sb[:], in0=lt_sb[:], scalar=-128.0, in1=lr_sb[:],
                            op0=ALU.add, op1=ALU.subtract),
                            [prev] + misc["xq"].wdeps(), "xqf")
                        misc["lt"].read(xqo); misc["xq"].wrote(xqo)

                        row0 = q0 + tt * 128
                        sp1 = dma("ST", scr["x1"][row0: row0 + 128, :], x1_sb[:], [x1o], "x1sp")
                        misc["x1"].read(sp1)
                        x1_spills.append(sp1)
                        sp2 = dma("ST", scr["xq"][row0: row0 + 128, :], xq_sb[:], [xqo], "xqsp")
                        misc["xq"].read(sp2)
                        xq_spills.append(sp2)
                        if "x1" in dbg:
                            d = dma("ST", dbg["x1"][row0: row0 + 128, :], x1_sb[:], [x1o], "dbgx1")
                            misc["x1"].read(d)
                        if "r" in dbg:
                            d = dma("ST", dbg["r"][row0: row0 + 128, :], r_sb[:], [v3], "dbgr")
                            misc["r"].read(d)

                    for j in range(16):
                        if j < 8:
                            c = 2 * qt + j // 4
                            s0 = (j % 4) * D
                            src = din["w1f_T"][c * 128:(c + 1) * 128, s0: s0 + D]
                            dst = scr["w1q"][c * 128:(c + 1) * 128, s0: s0 + D]
                            tp, tn = ld_th1, ld_th1n
                            splist = w1q_spill_ops[c]
                        else:
                            c = 8 * qt + (j - 8)
                            src = din["w2f_T"][c * 128:(c + 1) * 128, :]
                            dst = scr["w2q"][c * 128:(c + 1) * 128, :]
                            tp, tn = ld_th2, ld_th2n
                            splist = w2q_spill_ops[c]
                        cp_, cn_ = (10, 11) if j < 8 else (18, 19)
                        xt = xt_sb[j % 2]
                        xtb_ = xt_buf[j % 2]
                        ld = dma("G", xt[:], src, xtb_.wdeps(), f"wtld{qt}_{j}")
                        xtb_.wrote(ld)
                        ta = S.add("V", lambda e, xt=xt, cp=cp_: e.tensor_single_scalar(
                            out=lt_sb[:], in_=xt[:], scalar=scol[:, cp: cp + 1],
                            op=ALU.is_ge),
                            [ld, tp] + misc["lt"].wdeps(), "tern_a")
                        xtb_.read(ta); misc["lt"].wrote(ta)
                        tb_ = S.add("V", lambda e, xt=xt, cn=cn_: e.tensor_single_scalar(
                            out=lr_sb[:], in_=xt[:], scalar=scol[:, cn: cn + 1],
                            op=ALU.is_le),
                            [ld, tn] + misc["lr"].wdeps(), "tern_b")
                        xtb_.read(tb_); misc["lr"].wrote(tb_)
                        tq_ = S.add("V", lambda e: e.tensor_sub(tmpbf[:], lt_sb[:], lr_sb[:]),
                                    [ta, tb_] + tmp_buf.wdeps(), "tern_q")
                        misc["lt"].read(tq_); misc["lr"].read(tq_)
                        tmp_buf.wrote(tq_)
                        sp = dma("G", dst, tmpbf[:], [tq_], f"wtsp{qt}_{j}")
                        tmp_buf.read(sp)
                        splist.append(sp)

        S.fence()
        g_spills = []
        q1_ops = []
        with ExitStack() as S5a:
            sb5 = lambda *a, **k: S5a.enter_context(nc.sbuf_tensor(*a, **k))
            w1q_sb = sb5("w1q_sb", [128, 8 * H4], BF16)
            xqT_sb = sb5("xqT_sb", [128, 8 * TO], BF16, is_transpose_dest=True)
            b1_rep = sb5("b1_rep", [128, H4], F32)
            t_sb = sb5("t_sb", [128, H4], F32)
            ga_sb = [sb5(f"ga_sb{i}", [128, H4], F32) for i in range(2)]
            ga_buf = [Buf(), Buf()]
            t_buf = Buf()

            w1q_lds = []
            for c in range(8):
                ld = dma("G", w1q_sb[:, c * H4:(c + 1) * H4],
                         scr["w1q"][c * 128:(c + 1) * 128, :],
                         list(w1q_spill_ops[c]), f"w1qld{c}")
                w1q_lds.append(ld)
            ld_b1rep = bcast_load(b1_rep[:], din["b1f_r"], 0, H4, [], "b1rep")
            xqt_lds = []
            for kc in range(8):
                t = dma_t(xqT_sb[:, kc * TO:(kc + 1) * TO],
                          scr["xq"][:, kc * 128:(kc + 1) * 128],
                          list(xq_spills), "xqT")
                xqt_lds.append(t)
            d1o = S.add("V", lambda e: e.tensor_scalar(
                out=d1_all[:], in0=xs_all[:], scalar1=scol[:, 12:13], scalar2=None,
                op0=ALU.mult), [ld_d1f] + [x1_spills[-1]], "d1all")

            hps = [psum[i] for i in range(8)]
            for tt in range(16):
                mms_ntg = []
                for ntg in range(2):
                    mml = []
                    for kc in range(8):
                        for nt in range(4):
                            bi = ntg * 4 + nt
                            bb = hps[bi]
                            first = kc == 0
                            deps = [xqt_lds[kc], w1q_lds[kc]]
                            if first:
                                deps += bb.wdeps()
                            mm = S.add("T", lambda e, kc=kc, bi=bi, tt=tt: e.matmul(
                                bank(bi), xqT_sb[:, kc * TO + tt * 128: kc * TO + (tt + 1) * 128],
                                w1q_sb[:, kc * H4 + bi * 512: kc * H4 + (bi + 1) * 512],
                                start=(kc == 0), stop=(kc == 7)), deps, "h1mm")
                            if first:
                                bb.wrote(mm)
                            else:
                                bb.producer = mm
                            mml.append(mm)
                    mms_ntg.append(mml)
                evs = []
                for ntg in range(2):
                    deps = [mms_ntg[ntg][-1], d1o, ld_b1rep]
                    ev = S.add("V", lambda e, ntg=ntg, tt=tt: e.scalar_tensor_tensor(
                        out=t_sb[:, ntg * 2048:(ntg + 1) * 2048],
                        in0=PS[0:128, ntg * 4 * 512:(ntg + 1) * 4 * 512],
                        scalar=d1_all[:, tt: tt + 1],
                        in1=b1_rep[:, ntg * 2048:(ntg + 1) * 2048],
                        op0=ALU.mult, op1=ALU.add),
                        deps + (t_buf.wdeps() if ntg == 0 else []), "h1ev")
                    for nt in range(4):
                        hps[ntg * 4 + nt].read(ev)
                    evs.append(ev)
                t_buf.wrote(evs[-1])
                ga = ga_sb[tt % 2]
                gab = ga_buf[tt % 2]
                gl = S.add("A", lambda e, ga=ga: e.activation(
                    out=ga[:], in_=t_sb[:], func=ACT.Gelu),
                    [evs[0], evs[1]] + gab.wdeps(), "gelu")
                t_buf.read(gl); gab.wrote(gl)
                q1 = S.add("V", lambda e, ga=ga, tt=tt: e.tensor_reduce(
                    out=gs_all[:, tt: tt + 1], in_=ga[:], axis=AX, op=ALU.max,
                    apply_absolute_value=True), [gl], "gsr")
                gab.read(q1)
                q1_ops.append(q1)
                sp = dma("ST", scr["g"][tt * 128:(tt + 1) * 128, :], ga[:], [gl], "gsp")
                gab.read(sp)
                g_spills.append(sp)

        S.fence()
        out_stores = []
        with ExitStack() as S5b:
            sb6 = lambda *a, **k: S5b.enter_context(nc.sbuf_tensor(*a, **k))
            w2q_sb = sb6("w2q_sb", [128, 32 * D], BF16)
            gt_sb = [sb6(f"gt_sb{i}", [128, H4], F32) for i in range(2)]
            lt2 = sb6("lt2", [128, H4], F32)
            lr2 = sb6("lr2", [128, H4], F32)
            gq_sb = sb6("gq_sb", [128, H4], BF16)
            gqT_sb = sb6("gqT_sb", [128, 32 * 256], BF16, is_transpose_dest=True)
            x1t = [sb6(f"x1t{i}", [128, D], F32) for i in range(2)]
            rb = sb6("rb", [128, D], F32)
            wjunk2 = sb6("wjunk2", [128, D], BF16)
            b2f_rep = sb6("b2f_rep", [128, D], F32)
            gt_buf = [Buf(), Buf()]
            lt_buf = Buf(); lr_buf = Buf(); gq_buf = Buf(); gqt_buf = Buf()
            x1t_buf = [Buf(), Buf()]
            m6 = {nm: Buf() for nm in ("r", "bn6", "bn2", "sq")}

            w2q_lds = []
            for c in range(32):
                ld = dma("G", w2q_sb[:, c * D:(c + 1) * D],
                         scr["w2q"][c * 128:(c + 1) * 128, :],
                         list(w2q_spill_ops[c]), f"w2qld{c}")
                w2q_lds.append(ld)
            ld_b2frep = bcast_load(b2f_rep[:], din["b2f_r"], 0, D, [], "b2frep")

            c1o = S.add("V", lambda e: e.tensor_scalar_max(gs_all[:], gs_all[:], QEPS),
                        [], "gclip", force=[q1_ops[-1]])
            c2o = S.add("V", lambda e: e.reciprocal(g127_all[:], gs_all[:]),
                        [], "ginv", force=[c1o])
            c3o = S.add("V", lambda e: e.tensor_scalar_mul(g127_all[:], g127_all[:], 127.0),
                        [], "g127", force=[c2o])
            d2o = S.add("V", lambda e: e.tensor_scalar(
                out=d2_all[:], in0=gs_all[:], scalar1=scol[:, 20:21], scalar2=None,
                op0=ALU.mult), [ld_d2f], "d2all", force=[c1o])

            g_lds = {}

            def g_load(tt):
                d = dma("SY", gt_sb[tt % 2][:], scr["g"][tt * 128:(tt + 1) * 128, :],
                        [g_spills[tt]] + gt_buf[tt % 2].wdeps(), f"gld{tt}")
                gt_buf[tt % 2].wrote(d)
                g_lds[tt] = d

            g_load(0)
            g_load(1)

            gq_spill_l = {}

            def ladder_tile(tt):
                gt = gt_sb[tt % 2]
                gb = gt_buf[tt % 2]
                l1 = S.add("A", lambda e, gt=gt, tt=tt: e.activation(
                    out=lt2[:], in_=gt[:], func=ACT.Identity,
                    scale=g127_all[:, tt: tt + 1], bias=h128_sb[:]),
                    [g_lds[tt], c3o, op_h128] + lt_buf.wdeps(), "g_l0")
                gb.read(l1); lt_buf.wrote(l1)
                pu = S.add("V", lambda e, gt=gt: e.tensor_scalar(
                    out=gt[:], in0=lt2[:], scalar1=128.0, scalar2=128.0,
                    op0=ALU.is_ge, op1=ALU.mult), [l1] + gb.wdeps(), "g_lg")
                gb.wrote(pu)
                prev = S.add("V", lambda e, gt=gt: e.tensor_sub(lr2[:], lt2[:], gt[:]),
                             [pu] + lr_buf.wdeps(), "g_ls")
                gb.read(prev)
                for bbit in (64.0, 32.0, 16.0, 8.0, 4.0, 2.0, 1.0):
                    pu = S.add("V", lambda e, bb=bbit, gt=gt: e.tensor_scalar(
                        out=gt[:], in0=lr2[:], scalar1=bb, scalar2=bb,
                        op0=ALU.is_ge, op1=ALU.mult), [prev], "g_lg")
                    gb.wrote(pu)
                    prev = S.add("V", lambda e, gt=gt: e.tensor_sub(lr2[:], lr2[:], gt[:]),
                                 [pu], "g_ls")
                    gb.read(prev)
                lr_buf.wrote(prev)
                gqo = S.add("V", lambda e: e.scalar_tensor_tensor(
                    out=gq_sb[:], in0=lt2[:], scalar=-128.0, in1=lr2[:],
                    op0=ALU.add, op1=ALU.subtract), [prev] + gq_buf.wdeps(), "gqf")
                lt_buf.read(gqo); lr_buf.read(gqo); gq_buf.wrote(gqo)
                sp = dma("ST", scr["gq"][tt * 128:(tt + 1) * 128, :], gq_sb[:], [gqo], "gqsp")
                gq_buf.read(sp)
                gq_spill_l[tt] = sp
                if tt + 2 < 16:
                    g_load(tt + 2)

            def emit_group(grp):
                base = (grp % 2) * 4
                for tl in range(2):
                    ladder_tile(grp * 2 + tl)
                spgate = dma("ST", scr["col"][3:4, 0:2], stjunk[:],
                             [gq_spill_l[grp * 2], gq_spill_l[grp * 2 + 1], op_stj],
                             "gqgate")
                glds = []
                for hc in range(32):
                    deps = [spgate]
                    if hc == 0:
                        deps += gqt_buf.wdeps()
                    t = dma_t(gqT_sb[:, hc * 256:(hc + 1) * 256],
                              scr["gq"][grp * 256:(grp + 1) * 256, hc * 128:(hc + 1) * 128],
                              deps, "gqT")
                    glds.append(t)
                gqt_buf.wrote(glds[-1])
                mml_last = [None, None]
                for tl in range(2):
                    for hc in range(32):
                        for nt in range(2):
                            bi = base + tl * 2 + nt
                            bb = psum[bi]
                            first = hc == 0
                            deps = [glds[hc], w2q_lds[hc]]
                            if first:
                                deps += bb.wdeps()
                            mm = S.add("T", lambda e, hc=hc, tl=tl, nt=nt, bi=bi: e.matmul(
                                bank(bi),
                                gqT_sb[:, hc * 256 + tl * 128: hc * 256 + (tl + 1) * 128],
                                w2q_sb[:, hc * D + nt * 512: hc * D + (nt + 1) * 512],
                                start=(hc == 0), stop=(hc == 31)), deps, "h2mm")
                            gqt_buf.read(mm)
                            if first:
                                bb.wrote(mm)
                            else:
                                bb.producer = mm
                            mml_last[tl] = mm

                def epilogue(grp=grp, base=base, mml_last=mml_last):
                    for tl in range(2):
                        tt = grp * 2 + tl
                        xt = x1t[tt % 2]
                        xb_ = x1t_buf[tt % 2]
                        ldx = dma("SY", xt[:], scr["x1"][tt * 128:(tt + 1) * 128, :],
                                  xb_.wdeps() + [x1_spills[tt]], "x1ld")
                        ldg = dma("SY", syjunk[:], din["ident"][:, 2:4], [ldx], "x1gate")
                        xb_.wrote(ldx)
                        v1 = S.add("V", lambda e, xt=xt: e.tensor_add(rb[:], xt[:], b2f_rep[:]),
                                   [ldg, ld_b2frep] + m6["r"].wdeps(), "o_t0")
                        xb_.read(v1); m6["r"].wrote(v1)
                        bb0, bb1 = psum[base + tl * 2], psum[base + tl * 2 + 1]
                        v2 = S.add("V", lambda e, base=base, tl=tl, tt=tt: e.scalar_tensor_tensor(
                            out=rb[:], in0=PS[0:128, (base + tl * 2) * 512:(base + tl * 2 + 2) * 512],
                            scalar=d2_all[:, tt: tt + 1], in1=rb[:],
                            op0=ALU.mult, op1=ALU.add),
                            [mml_last[tl], v1, d2o], "o_r")
                        bb0.read(v2); bb1.read(v2)
                        m6["r"].wrote(v2)
                        s1 = S.add("A", lambda e: e.activation(
                            out=wjunk2[:], in_=rb[:], func=ACT.Identity,
                            accum_out=scol[:, 9:10]),
                            [v2] + m6["bn6"].wdeps(), "ln3s")
                        s2 = S.add("A", lambda e: e.activation(
                            out=wjunk2[:], in_=rb[:], func=ACT.Square,
                            accum_out=scol[:, 13:14]),
                            [v2], "ln3q")
                        m6["bn6"].wrote(s2)
                        m1 = S.add("V", lambda e: e.tensor_scalar_mul(
                            scol[:, 14:15], scol[:, 9:10], 1.0 / 1024.0),
                            [s1, s2] + m6["bn2"].wdeps(), "ln3m1")
                        m2 = S.add("V", lambda e: e.tensor_mul(
                            scol[:, 15:16], scol[:, 14:15], scol[:, 14:15]), [], "ln3m2", force=[m1])
                        m3 = S.add("V", lambda e: e.tensor_scalar_mul(
                            scol[:, 23:24], scol[:, 13:14], 1.0 / 1024.0), [s2], "ln3m3")
                        m4 = S.add("V", lambda e: e.tensor_sub(
                            scol[:, 23:24], scol[:, 23:24], scol[:, 15:16]), [], "ln3m4", force=[m2, m3])
                        m6["bn2"].wrote(m4)
                        a1 = S.add("A", lambda e: e.activation(
                            out=scol[:, 21:22], in_=scol[:, 23:24], func=ACT.Sqrt, bias=eps_sb[:, 1:2]),
                            [m4, op_eps] + m6["sq"].wdeps(), "o_sq")
                        m6["bn2"].read(a1); m6["sq"].wrote(a1)
                        r1 = S.add("V", lambda e: e.reciprocal(scol[:, 22:23], scol[:, 21:22]),
                                   [a1], "o_rc")
                        m6["sq"].read(r1)
                        oo = S.add("V", lambda e, xt=xt: e.tensor_scalar(
                            out=xt[:], in0=rb[:], scalar1=scol[:, 14:15], scalar2=scol[:, 22:23],
                            op0=ALU.subtract, op1=ALU.mult),
                            [m4] + xb_.wdeps(), "o_out", force=[r1])
                        m6["r"].read(oo); xb_.wrote(oo)
                        st = dma("ST", y_out[tt * 128:(tt + 1) * 128, :], xt[:], [oo], "ystore")
                        xb_.read(st)
                        out_stores.append(st)
                return epilogue

            pend = None
            for grp in range(8):
                nxt = emit_group(grp)
                if pend is not None:
                    pend()
                pend = nxt
            pend()

        S.add("T", lambda e: e.nop(), out_stores, "final")

        S.finalize()
        S.emit(nc, sems)
    return nc


DEBUG_SPECS = {
    "z": ([16 * 64, TO], F32),
    "x1": ([TO, D], F32),
    "r": ([TO, D], F32),
    "P": ([4 * 128, 8 * 512], F32),
}



def make_in_maps(L0_attn_x, L0_attn_W1, L0_attn_b1, L0_attn_W2, L0_attn_b2,
                 L0_attn_gamma, L0_attn_beta, L0_ffn_W1, L0_ffn_b1, L0_ffn_W2,
                 L0_ffn_b2):
    f = np.float32
    c = np.ascontiguousarray
    w1a = c(L0_attn_W1, dtype=f)
    b1a = c(L0_attn_b1, dtype=f)
    w2a = c(L0_attn_W2, dtype=f)
    b2a = c(L0_attn_b2, dtype=f)
    gam = c(L0_attn_gamma, dtype=f)
    bet = c(L0_attn_beta, dtype=f)
    w1f = c(L0_ffn_W1, dtype=f)
    b1f = c(L0_ffn_b1, dtype=f)
    w2f = c(L0_ffn_W2, dtype=f)
    b2f = c(L0_ffn_b2, dtype=f)

    shared = {
        "w1a": w1a,
        "b1a_c": c(b1a.reshape(32, 128).T),
        "b1a_r": c(b1a.reshape(1, H4)),
        "w2a": w2a,
        "gam_c": c(gam.reshape(8, 128).T),
        "bet_c": c(bet.reshape(8, 128).T),
        "b2a_r": c(b2a.reshape(1, D)),
        "w1f_T": c(w1f.T),
        "b1f_r": c(b1f.reshape(1, H4)),
        "w2f_T": c(w2f.T),
        "b2f_r": c(b2f.reshape(1, D)),
        "ident": np.eye(128, dtype=f),
    }
    x = np.asarray(L0_attn_x, dtype=f)
    in_maps = []
    for core in range(8):
        b, h = core // 2, core % 2
        own = c(x[b, h * TO:(h + 1) * TO, :])
        oth = c(x[b, (1 - h) * TO:(2 - h) * TO, :])
        m = dict(shared)
        m["x_own"] = own
        m["xT_own"] = c(own.T)
        m["xT_oth"] = c(oth.T)
        in_maps.append(m)
    return in_maps


def kernel(**inputs):
    nc = build_program()
    in_maps = make_in_maps(**inputs)
    res = run_bass_kernel_spmd(nc, in_maps, core_ids=list(range(8)))
    out = np.empty((4, FS, D), dtype=np.float32)
    for core in range(8):
        b, h = core // 2, core % 2
        out[b, h * TO:(h + 1) * TO, :] = res.results[core]["y"]
    return out


# version 09
# speedup vs baseline: 1.2334x; 1.2334x over previous
import math
from contextlib import ExitStack

import numpy as np
import ml_dtypes

import concourse.bass as bass
import concourse.mybir as mybir
from concourse.ap import AP
from concourse.bass_utils import run_bass_kernel_spmd

F32 = mybir.dt.float32
BF16 = mybir.dt.bfloat16
ALU = mybir.AluOpType
ACT = mybir.ActivationFunctionType
AX = mybir.AxisListType.X

D = 1024
FS = 4096
TO = 2048
H4 = 4096
NHP = 8
LN_EPS = 1e-5
QEPS = 1e-5


ENGINES = ("T", "A", "V", "SY", "G", "ST")
DMA_ENGS = {"SY", "G", "ST"}
BODY_OF = {"T": "tensor", "A": "scalar", "V": "vector",
           "SY": "sync", "G": "gpsimd", "ST": "scalar"}


class Op:
    __slots__ = ("eng", "fn", "deps", "force", "idx", "inc", "seq", "tag")

    def __init__(self, eng, fn, deps, force, tag=""):
        self.eng = eng
        self.fn = fn
        self.deps = deps
        self.force = force
        self.idx = -1
        self.inc = False
        self.seq = 0
        self.tag = tag


class Sched:
    def __init__(self):
        self.ops = []
        self.last = {}
        self.pending_fence = {}

    def add(self, eng, fn, deps=(), tag="", force=()):
        deps = [d for d in deps if d is not None]
        force = [d for d in force if d is not None]
        fence = self.pending_fence.pop(eng, None)
        if fence:
            deps = deps + [d for d in fence if d.eng != eng]
        op = Op(eng, fn, deps, force, tag)
        op.idx = len(self.ops)
        self.ops.append(op)
        self.last[eng] = op
        return op

    def fence(self):
        bar = list(self.last.values())
        for e in ENGINES:
            self.pending_fence[e] = bar

    def finalize(self):
        for op in self.ops:
            for d in op.deps:
                same_body = BODY_OF[d.eng] == BODY_OF[op.eng]
                if d.eng in DMA_ENGS or op.eng in DMA_ENGS or not same_body:
                    d.inc = True
            for d in op.force:
                d.inc = True
        cnt = {e: 0 for e in ENGINES}
        for op in self.ops:
            if op.inc:
                cnt[op.eng] += 16 if op.eng in DMA_ENGS else 1
            op.seq = cnt[op.eng]

    def emit(self, nc, sems):
        bodies = {}
        for op in self.ops:
            bodies.setdefault(BODY_OF[op.eng], []).append(op)

        def make_body(ops_for_body):
            def body(e):
                waited = {en: 0 for en in ENGINES}
                for op in ops_for_body:
                    for d in op.deps:
                        if (BODY_OF[d.eng] == BODY_OF[op.eng]
                                and d.eng not in DMA_ENGS
                                and op.eng not in DMA_ENGS):
                            continue
                        if d.seq > waited[d.eng]:
                            e.wait_ge(sems[d.eng], d.seq)
                            waited[d.eng] = d.seq
                    for d in op.force:
                        if d.seq > waited[d.eng]:
                            e.wait_ge(sems[d.eng], d.seq)
                            waited[d.eng] = d.seq
                    ins = op.fn(e)
                    if op.inc:
                        ins.then_inc(sems[op.eng], 16 if op.eng in DMA_ENGS else 1)
            return body

        blk = nc.Block()
        with blk:
            for name in ("sync", "gpsimd", "tensor", "scalar", "vector"):
                if name in bodies:
                    getattr(blk, name)(make_body(bodies[name]))


class Buf:
    __slots__ = ("producer", "readers")

    def __init__(self):
        self.producer = None
        self.readers = []

    def wdeps(self):
        d = list(self.readers)
        if self.producer is not None:
            d.append(self.producer)
        return d

    def rdeps(self):
        return [self.producer] if self.producer is not None else []

    def wrote(self, op):
        self.producer = op
        self.readers = []

    def read(self, op):
        self.readers.append(op)



def build_program(debug=(), expose=()):
    nc = bass.Bass("TRN2", target_bir_lowering=False, detect_race_conditions=False)
    S = Sched()

    din = {}
    for nm, shp in [
        ("x_own", [TO, D]), ("xT_own", [D, TO]), ("xT_oth", [D, TO]),
        ("w1a", [D, H4]), ("b1a_c", [128, 32]), ("b1a_r", [1, H4]),
        ("w2a", [D, D]),
        ("gam_c", [128, 8]), ("bet_c", [128, 8]),
        ("b2a_r", [1, D]),
        ("w1f_T", [D, H4]), ("b1f_r", [1, H4]),
        ("w2f_T", [H4, D]), ("b2f_r", [1, D]),
        ("ident", [128, 128]),
    ]:
        din[nm] = nc.dram_tensor(nm, shp, F32, kind="ExternalInput").ap()

    y_out = nc.dram_tensor("y", [TO, D], F32, kind="ExternalOutput").ap()

    dbg = {}
    for nm in debug:
        shp, dt = DEBUG_SPECS[nm]
        dbg[nm] = nc.dram_tensor("dbg_" + nm, shp, dt, kind="ExternalOutput").ap()

    scr = {}
    kind = lambda nm: "ExternalOutput" if nm in expose else "Internal"
    scr["kT"] = nc.dram_tensor("kT_d", [D, FS], BF16, kind=kind("kT")).ap()
    scr["v"] = nc.dram_tensor("v_d", [FS, D], BF16, kind=kind("v")).ap()
    scr["qT"] = nc.dram_tensor("qT_d", [D, TO], BF16, kind=kind("qT")).ap()
    scr["uT"] = nc.dram_tensor("uT_d", [D, TO], F32, kind=kind("uT")).ap()
    scr["x1"] = nc.dram_tensor("x1_d", [TO, D], F32, kind=kind("x1")).ap()
    scr["xq"] = nc.dram_tensor("xq_d", [TO, D], BF16, kind=kind("xq")).ap()
    scr["gq"] = nc.dram_tensor("gq_d", [TO, H4], BF16, kind=kind("gq")).ap()
    scr["g"] = nc.dram_tensor("g_d", [TO, H4], F32, kind=kind("g")).ap()
    scr["w1q"] = nc.dram_tensor("w1q_d", [D, H4], BF16, kind=kind("w1q")).ap()
    scr["w2q"] = nc.dram_tensor("w2q_d", [H4, D], BF16, kind=kind("w2q")).ap()
    scr["row"] = nc.dram_tensor("row_d", [4, H4], F32, kind=kind("row")).ap()
    scr["col"] = nc.dram_tensor("col_d", [4, 128], F32, kind=kind("col")).ap()

    with ExitStack() as TOP:
        sems = {e: TOP.enter_context(nc.semaphore("sem" + e)) for e in ENGINES}

        sb = lambda *a, **k: TOP.enter_context(nc.sbuf_tensor(*a, **k))
        ones_bf = sb("ones_bf", [128, 1], BF16)
        identity = sb("identity", [128, 128], F32)
        b1a_c = sb("b1a_cs", [128, 32], F32)
        gam_c = sb("gam_cs", [128, 8], F32)
        bet_c = sb("bet_cs", [128, 8], F32)
        xs_all = sb("xs_all", [128, 16], F32)
        gs_all = sb("gs_all", [128, 16], F32)
        d1_all = sb("d1_all", [128, 16], F32)
        d2_all = sb("d2_all", [128, 16], F32)
        g127_all = sb("g127_all", [128, 16], F32)
        srow = sb("srow", [1, 8 * 512], F32)
        scol = sb("scol", [128, 24], F32)
        bn6 = sb("bn6", [128, 12], F32)
        bn2 = sb("bn2", [128, 2], F32)

        PS = TOP.enter_context(nc.psum_tensor("PS", [128, 4096], F32))
        psum = [Buf() for _ in range(8)]
        def bank(i, lo=0, width=512):
            return PS[lo:128, i * 512: i * 512 + width]

        def dma(eng, out_ap, in_ap, deps, tag="", slow=False):
            return S.add(eng, lambda e, o=out_ap, i=in_ap, s=slow: e.dma_start(
                out=o, in_=i, allow_slow_non_contiguous=s), deps, tag)

        gjunk = sb("gjunk", [128, 2], BF16)
        syjunk = sb("syjunk", [128, 2], F32)

        def gdma_gated(out_ap, in_ap, deps, tag=""):
            d1 = dma("G", out_ap, in_ap, deps, tag)
            d2 = dma("G", gjunk[:], din["ident"][:, 0:2], [d1], tag + "_gate")
            return d2

        def dma_t(out_ap, in_ap, deps, tag=""):
            d1 = S.add("SY", lambda e, o=out_ap, i=in_ap: e.dma_start_transpose(out=o, in_=i),
                       deps, tag)
            d2 = S.add("SY", lambda e: e.dma_start(out=syjunk[:], in_=din["ident"][:, 2:4]),
                       [d1], tag + "_gate")
            return d2

        def bcast_load(col_ap, dram_t, off, n, deps, tag="bc"):
            src = AP(tensor=dram_t.tensor, offset=off, ap=[[0, 128], [1, n]])
            return dma("SY", col_ap, src, deps, tag)

        ld_ident = dma("SY", identity[:], din["ident"], [])
        ld_b1ac = dma("SY", b1a_c[:], din["b1a_c"], [])
        ld_gam = dma("SY", gam_c[:], din["gam_c"], [])
        ld_bet = dma("SY", bet_c[:], din["bet_c"], [])
        op_ones = S.add("V", lambda e: e.memset(ones_bf[:], 1.0), [])
        eps_sb = sb("eps_sb", [128, 2], F32)
        h128_sb = sb("h128_sb", [128, 1], F32)
        op_h128 = S.add("V", lambda e: e.memset(h128_sb[:], 128.5), [])
        stjunk = sb("stjunk", [1, 2], F32)
        op_stj = S.add("V", lambda e: e.memset(stjunk[:], 0.0), [])
        op_eps0 = S.add("V", lambda e: e.memset(eps_sb[:, 0:1], 100000.0), [])
        op_eps = S.add("V", lambda e: e.reciprocal(eps_sb[:, 1:2], eps_sb[:, 0:1]),
                       [], force=[op_eps0])
        for wf in (ACT.Silu, ACT.Sqrt, ACT.Gelu, ACT.Square, ACT.Identity, ACT.Abs):
            S.add("A", lambda e, wf=wf: e.activation(
                out=bn6[:, 0:1], in_=eps_sb[:, 1:2], func=wf), [op_eps], "warm")

        with ExitStack() as S1:
            sb1 = lambda *a, **k: S1.enter_context(nc.sbuf_tensor(*a, **k))
            w1bf = sb1("w1bf", [128, 8 * H4], BF16)
            xtb = [sb1(f"xtb{i}", [128, 8 * 512], BF16) for i in range(2)]
            xtb_buf = [Buf(), Buf()]
            stg_f = [sb1(f"stgf{i}", [128, 512], F32) for i in range(2)]
            stg_b = [sb1(f"stgb{i}", [128, 512], BF16) for i in range(4)]
            stgv_f = [sb1(f"stgvf{i}", [128, 1024], F32) for i in range(2)]
            stgv_b = [sb1(f"stgvb{i}", [128, 1024], BF16) for i in range(2)]
            b1v_rep = sb1("b1v_rep", [128, D], F32)
            stg_f_buf = [Buf(), Buf()]
            stg_b_buf = [Buf() for _ in range(4)]
            stgv_f_buf = [Buf(), Buf()]
            stgv_b_buf = [Buf(), Buf()]

            w1_src = din["w1a"].rearrange("(kc p) c -> p kc c", p=128)
            ld_w1bf = gdma_gated(w1bf.ap().rearrange("p (kc c) -> p kc c", kc=8),
                                 w1_src, [], "w1bf")
            ld_b1v = bcast_load(b1v_rep[:], din["b1a_r"], D, D, [], "b1v")

            fcnt = [0]
            kqv_spill_ops = []

            stg_w = [sb1(f"stgw{i}", [128, H4], F32) for i in range(2)]
            stg_wj = sb1("stgwj", [128, H4], BF16)
            stg_w_buf = [Buf(), Buf()]
            stg_wj_buf = Buf()
            wacc_buf = [Buf(), Buf()]
            z1 = S.add("V", lambda e: e.memset(scol[:, 8:9], 0.0), [], "wacc0a")
            wacc_buf[0].wrote(z1)
            z2 = S.add("V", lambda e: e.memset(scol[:, 16:17], 0.0), [], "wacc0b")
            wacc_buf[1].wrote(z2)

            def wstat_load(i):
                if i < 8:
                    src = din["w1f_T"][i * 128:(i + 1) * 128, :]
                    d = dma("SY", stg_w[i % 2][:], src, stg_w_buf[i % 2].wdeps(),
                            f"wstld{i}")
                else:
                    g = i - 8
                    src = AP(tensor=din["w2f_T"].tensor, offset=g * 4 * 128 * D,
                             ap=[[D, 128], [128 * D, 4], [1, D]])
                    d = dma("SY", stg_w[i % 2].ap().rearrange("p (c o) -> p c o", c=4),
                            src, stg_w_buf[i % 2].wdeps(), f"wstld{i}")
                stg_w_buf[i % 2].wrote(d)
                return d

            wstat_lds = [wstat_load(0), wstat_load(1)]

            def wstat_step(i):
                w2 = i >= 8
                acol, scol_a = (16, 17) if w2 else (8, 9)
                ab = S.add("A", lambda e, a=scol_a: e.activation(
                    out=stg_wj[:], in_=stg_w[i % 2][:], func=ACT.Abs,
                    accum_out=scol[:, a: a + 1]),
                    [wstat_lds[i]] + stg_wj_buf.wdeps() + wacc_buf[w2].rdeps(),
                    f"wabs{i}")
                stg_w_buf[i % 2].read(ab)
                stg_wj_buf.wrote(ab)
                ac = S.add("V", lambda e, a=acol, b=scol_a: e.tensor_add(
                    scol[:, a: a + 1], scol[:, a: a + 1], scol[:, b: b + 1]),
                    [ab] + wacc_buf[w2].wdeps(), f"wacs{i}")
                wacc_buf[w2].wrote(ac)
                stg_wj_buf.read(ac)
                if i + 2 < 16:
                    wstat_lds.append(wstat_load(i + 2))

            wmm = S.add("T", lambda e: e.matmul(
                bank(0)[0:128, 0:128], identity[:], identity[:],
                start=True, stop=True), [ld_ident] + psum[0].wdeps(), "warmmm")
            psum[0].wrote(wmm)
            wev = S.add("A", lambda e: e.activation(
                out=bn6[:, 1:2], in_=bank(0)[0:128, 0:1], func=ACT.Identity),
                psum[0].rdeps(), "warmev")
            psum[0].read(wev)

            def p1_block(tb):
                src = din["xT_own"] if tb < 4 else din["xT_oth"]
                t0 = (tb % 4) * 512
                xa = xtb[tb % 2]
                xbuf = xtb_buf[tb % 2]
                sap = AP(tensor=src.tensor, offset=t0,
                         ap=[[TO, 128], [128 * TO, 8], [1, 512]])
                ld = gdma_gated(xa.ap().rearrange("p (kc t) -> p kc t", kc=8), sap,
                                xbuf.wdeps(), f"xtb{tb}")
                xbuf.wrote(ld)

                ocs = []
                if tb < 4:
                    ocs += [("u", oc) for oc in range(8)]
                    ocs += [("q", oc) for oc in range(8)]
                ocs += [("k", oc) for oc in range(8)]
                for kind, oc in ocs:
                    base_col = {"u": 0, "q": 2048, "k": 3072}[kind] + oc * 128
                    bname = fcnt[0] % 4
                    bb = psum[bname]
                    mms = []
                    for kc in range(8):
                        deps = xbuf.rdeps() + [ld_w1bf] + (bb.wdeps() if kc == 0 else [])
                        mm = S.add("T", lambda e, kc=kc, bn=bname, bc=base_col, xa=xa: e.matmul(
                            bank(bn), w1bf[:, kc * H4 + bc: kc * H4 + bc + 128],
                            xa[:, kc * 512: (kc + 1) * 512],
                            start=(kc == 0), stop=(kc == 7)),
                            deps, f"p1mm{kind}{oc}")
                        mms.append(mm)
                        xbuf.read(mm)
                    bb.wrote(mms[-1])
                    cchunk = {"u": oc, "q": 16 + oc, "k": 24 + oc}[kind]
                    if kind == "u":
                        st = stg_f[fcnt[0] % 2]
                        stb = stg_f_buf[fcnt[0] % 2]
                        odt = st
                    else:
                        st = stg_b[fcnt[0] % 4]
                        stb = stg_b_buf[fcnt[0] % 4]
                        odt = st
                    ev = S.add("A", lambda e, bn=bname, st=odt, cc=cchunk: e.activation(
                        out=st[:], in_=bank(bn), func=ACT.Silu,
                        bias=b1a_c[:, cc: cc + 1], scale=1.0),
                        bb.rdeps() + stb.wdeps() + [ld_b1ac], f"p1ev{kind}{oc}")
                    bb.read(ev)
                    stb.wrote(ev)
                    if kind == "u":
                        dst = scr["uT"][oc * 128:(oc + 1) * 128, t0: t0 + 512]
                    elif kind == "q":
                        dst = scr["qT"][oc * 128:(oc + 1) * 128, t0: t0 + 512]
                    else:
                        dst = scr["kT"][oc * 128:(oc + 1) * 128, tb * 512:(tb + 1) * 512]
                    sp = dma("ST", dst, odt[:], stb.rdeps(), f"p1sp{kind}{oc}")
                    stb.read(sp)
                    kqv_spill_ops.append(sp)
                    fcnt[0] += 1

                for tl in range(4):
                    bsel = 4 + (tl % 2) * 2
                    bb0, bb1 = psum[bsel], psum[bsel + 1]
                    mms = []
                    for kc in range(8):
                        for nt in range(2):
                            bb = (bb0, bb1)[nt]
                            deps = xbuf.rdeps() + [ld_w1bf] + (bb.wdeps() if kc == 0 else [])
                            mm = S.add("T", lambda e, kc=kc, nt=nt, tl=tl, bs=bsel, xa=xa: e.matmul(
                                bank(bs + nt),
                                xa[:, kc * 512 + tl * 128: kc * 512 + tl * 128 + 128],
                                w1bf[:, kc * H4 + 1024 + nt * 512: kc * H4 + 1024 + (nt + 1) * 512],
                                start=(kc == 0), stop=(kc == 7)),
                                deps, "p1vmm")
                            mms.append(mm)
                            xbuf.read(mm)
                            if kc == 0:
                                bb.wrote(mm)
                            else:
                                bb.producer = mm
                    vf = stgv_f[tl % 2]
                    vfb = stgv_f_buf[tl % 2]
                    addv = S.add("V", lambda e, bs=bsel, vf=vf: e.tensor_add(
                        vf[:], PS[0:128, bs * 512:(bs + 2) * 512], b1v_rep[:]),
                        bb0.rdeps() + bb1.rdeps() + vfb.wdeps() + [ld_b1v], "p1vadd")
                    bb0.read(addv); bb1.read(addv); vfb.wrote(addv)
                    vb = stgv_b[tl % 2]
                    vbb = stgv_b_buf[tl % 2]
                    sil = S.add("A", lambda e, vf=vf, vb=vb: e.activation(
                        out=vb[:], in_=vf[:], func=ACT.Silu),
                        vfb.rdeps() + vbb.wdeps(), "p1vsil")
                    vfb.read(sil); vbb.wrote(sil)
                    sp = dma("ST", scr["v"][tb * 512 + tl * 128: tb * 512 + (tl + 1) * 128, :],
                             vb[:], vbb.rdeps(), "p1vsp")
                    vbb.read(sp)
                    kqv_spill_ops.append(sp)

            for tb in range(8):
                p1_block(tb)
                wstat_step(tb * 2)
                wstat_step(tb * 2 + 1)

            spc1 = dma("ST", scr["col"][0:1, :], scol[:, 8:9], wacc_buf[0].rdeps(), "accsp1")
            wacc_buf[0].read(spc1)
            ldr1 = dma("SY", srow[0:1, 0:128], scr["col"][0:1, :], [spc1], "accld1")
            rs1 = S.add("V", lambda e: e.reduce_sum(srow[0:1, 128:129], srow[0:1, 0:128], AX),
                        [ldr1], "wsum1")
            ws1o = S.add("V", lambda e: e.tensor_scalar(
                out=srow[0:1, 129:130], in0=srow[0:1, 128:129],
                scalar1=1.0 / (D * H4), scalar2=QEPS, op0=ALU.mult, op1=ALU.max),
                [], "ws1", force=[rs1])
            th1 = S.add("V", lambda e: e.tensor_scalar_mul(srow[0:1, 130:131], srow[0:1, 129:130], 0.5),
                        [], "thr1", force=[ws1o])
            th1n = S.add("V", lambda e: e.tensor_scalar_mul(srow[0:1, 131:132], srow[0:1, 130:131], -1.0),
                         [], "thr1n", force=[th1])
            d1f = S.add("V", lambda e: e.tensor_scalar_mul(srow[0:1, 132:133], srow[0:1, 129:130], 1.0 / 127.0),
                        [], "d1f", force=[ws1o])
            spr1 = dma("ST", scr["row"][3:4, 0:5], srow[0:1, 128:133], [th1n, d1f], "rowsp1")
            ld_th1 = bcast_load(scol[:, 10:11], scr["row"], 3 * H4 + 2, 1, [spr1], "th1c")
            ld_th1n = bcast_load(scol[:, 11:12], scr["row"], 3 * H4 + 3, 1, [spr1], "th1nc")
            ld_d1f = bcast_load(scol[:, 12:13], scr["row"], 3 * H4 + 4, 1, [spr1], "d1fc")
            spc2 = dma("ST", scr["col"][1:2, :], scol[:, 16:17], wacc_buf[1].rdeps(), "accsp2")
            wacc_buf[1].read(spc2)
            ldr2 = dma("SY", srow[0:1, 256:384], scr["col"][1:2, :], [spc2], "accld2")
            rs2 = S.add("V", lambda e: e.reduce_sum(srow[0:1, 200:201], srow[0:1, 256:384], AX),
                        [ldr2], "wsum2")
            ws2o = S.add("V", lambda e: e.tensor_scalar(
                out=srow[0:1, 201:202], in0=srow[0:1, 200:201],
                scalar1=1.0 / (D * H4), scalar2=QEPS, op0=ALU.mult, op1=ALU.max),
                [], "ws2", force=[rs2])
            th2 = S.add("V", lambda e: e.tensor_scalar_mul(srow[0:1, 202:203], srow[0:1, 201:202], 0.5),
                        [], "thr2", force=[ws2o])
            th2n = S.add("V", lambda e: e.tensor_scalar_mul(srow[0:1, 203:204], srow[0:1, 202:203], -1.0),
                         [], "thr2n", force=[th2])
            d2f = S.add("V", lambda e: e.tensor_scalar_mul(srow[0:1, 204:205], srow[0:1, 201:202], 1.0 / 127.0),
                        [], "d2f", force=[ws2o])
            spr2 = dma("ST", scr["row"][3:4, 8:13], srow[0:1, 200:205], [th2n, d2f], "rowsp2")
            ld_th2 = bcast_load(scol[:, 18:19], scr["row"], 3 * H4 + 10, 1, [spr2], "th2c")
            ld_th2n = bcast_load(scol[:, 19:20], scr["row"], 3 * H4 + 11, 1, [spr2], "th2nc")
            ld_d2f = bcast_load(scol[:, 20:21], scr["row"], 3 * H4 + 12, 1, [spr2], "d2fc")

        S.fence()
        with ExitStack() as S2:
            sb2 = lambda *a, **k: S2.enter_context(nc.sbuf_tensor(*a, **k))
            z_all = sb2("z_all", [128, 8 * TO], F32)
            z_buf = [Buf() for _ in range(8)]

            with ExitStack() as S2a:
                sb2a = lambda *a, **k: S2a.enter_context(nc.sbuf_tensor(*a, **k))
                NB = 2
                k_sb = [sb2a(f"k_sb{i}", [128, FS], BF16) for i in range(NB)]
                qm_he = [sb2a(f"qm_he{i}", [128, TO], BF16) for i in range(NB)]
                qm_ho = [sb2a(f"qm_ho{i}", [128, TO], BF16) for i in range(NB)]
                u_sb = [sb2a(f"u_sb{i}", [128, TO], F32) for i in range(NB)]
                vm_sb = [sb2a(f"vm_sb{i}", [128, 2 * FS], BF16) for i in range(NB)]
                sc_sb = [sb2a(f"sc_sb{i}", [128, 1024], BF16) for i in range(2)]
                ld_bufs = {nm: [Buf() for _ in range(NB)]
                           for nm in ("k_sb", "qm_he", "qm_ho", "u_sb", "vm_sb")}
                sc_bufs = [Buf(), Buf()]

                for i in range(NB):
                    zm = S.add("V", lambda e, i=i: e.memset(vm_sb[i][:], 0.0), [], "vmzero")
                    ld_bufs["vm_sb"][i].wrote(zm)
                    z1 = S.add("V", lambda e, i=i: e.memset(qm_he[i][64:128, :], 0.0), [], "qmz1")
                    ld_bufs["qm_he"][i].wrote(z1)
                    z2 = S.add("V", lambda e, i=i: e.memset(qm_ho[i][0:64, :], 0.0), [], "qmz2")
                    ld_bufs["qm_ho"][i].wrote(z2)

                for hp in range(8):
                    sl = hp % NB
                    loads = {}
                    loads["k_sb"] = dma("SY", k_sb[sl][:], scr["kT"][hp * 128:(hp + 1) * 128, :],
                                        ld_bufs["k_sb"][sl].wdeps() + (kqv_spill_ops if hp == 0 else []),
                                        f"k{hp}")
                    loads["qm_he"] = dma("SY", qm_he[sl][0:64, :], scr["qT"][hp * 128: hp * 128 + 64, :],
                                         ld_bufs["qm_he"][sl].wdeps(), f"q_he{hp}")
                    loads["qm_ho"] = dma("SY", qm_ho[sl][64:128, :], scr["qT"][hp * 128 + 64: hp * 128 + 128, :],
                                         ld_bufs["qm_ho"][sl].wdeps(), f"q_ho{hp}")
                    loads["u_sb"] = dma("SY", u_sb[sl][:], scr["uT"][hp * 128:(hp + 1) * 128, :],
                                        ld_bufs["u_sb"][sl].wdeps(), f"u{hp}")
                    vdeps = ld_bufs["vm_sb"][sl].wdeps()
                    src_he = AP(tensor=scr["v"].tensor, offset=hp * 128,
                                ap=[[D, 128], [128 * D, 32], [1, 64]])
                    dst_he = AP(tensor=vm_sb[sl], offset=0,
                                ap=[[2 * FS, 128], [256, 32], [1, 64]])
                    l1 = dma("SY", dst_he, src_he, vdeps, f"vmhe{hp}")
                    src_ho = AP(tensor=scr["v"].tensor, offset=hp * 128 + 64,
                                ap=[[D, 128], [128 * D, 32], [1, 64]])
                    dst_ho = AP(tensor=vm_sb[sl], offset=192,
                                ap=[[2 * FS, 128], [256, 32], [1, 64]])
                    l2 = dma("SY", dst_ho, src_ho, vdeps, f"vmho{hp}")
                    ld_bufs["vm_sb"][sl].producer = l2
                    ld_bufs["vm_sb"][sl].readers.append(l1)
                    for nm in ("k_sb", "qm_he", "qm_ho", "u_sb"):
                        ld_bufs[nm][sl].wrote(loads[nm])
                    vm_lds = [l1, l2]
                    kq = {"k": k_sb[sl], "he": qm_he[sl], "ho": qm_ho[sl]}

                    for qt in range(4):
                        q0 = qt * 512
                        avb = psum[4]
                        sils = [None] * 32

                        def emit_sc(kt):
                            par = kt % 2
                            smm = []
                            for half, b in (("he", 0), ("ho", 1)):
                                bb = psum[par * 2 + b]
                                mm = S.add("T", lambda e, kt=kt, par=par, b=b, half=half, kq=kq, q0=q0: e.matmul(
                                    bank(par * 2 + b),
                                    kq["k"][:, kt * 128:(kt + 1) * 128],
                                    kq[half][:, q0: q0 + 512], start=True, stop=True),
                                    ld_bufs["k_sb"][sl].rdeps()
                                    + ld_bufs["qm_" + half][sl].rdeps() + bb.wdeps(),
                                    f"sc{half}")
                                ld_bufs["k_sb"][sl].read(mm)
                                ld_bufs["qm_" + half][sl].read(mm)
                                bb.wrote(mm)
                                smm.append(mm)
                            sil = S.add("A", lambda e, par=par: e.activation(
                                out=sc_sb[par][:], in_=PS[0:128, par * 1024:(par + 1) * 1024],
                                func=ACT.Silu),
                                smm + sc_bufs[par].wdeps(), "sil")
                            psum[par * 2].read(sil)
                            psum[par * 2 + 1].read(sil)
                            sc_bufs[par].wrote(sil)
                            sils[kt] = sil

                        def emit_av(kt):
                            par = kt % 2
                            for half, moff, off in (("he", 0, 0), ("ho", 128, 512)):
                                first = (kt == 0 and half == "he")
                                last = (kt == 31 and half == "ho")
                                deps = [sils[kt]] + vm_lds
                                if first:
                                    deps = deps + avb.wdeps()
                                mm = S.add("T", lambda e, kt=kt, par=par, moff=moff, off=off, f=first, l=last, sl=sl: e.matmul(
                                    bank(4), vm_sb[sl][:, kt * 256 + moff: kt * 256 + moff + 128],
                                    sc_sb[par][:, off: off + 512],
                                    start=f, stop=l), deps, f"av{half}")
                                sc_bufs[par].read(mm)
                                ld_bufs["vm_sb"][sl].read(mm)
                                if first:
                                    avb.wrote(mm)
                                else:
                                    avb.producer = mm

                        emit_sc(0)
                        for kt in range(1, 32):
                            emit_sc(kt)
                            emit_av(kt - 1)
                        emit_av(31)

                        st = S.add("V", lambda e, hp=hp, q0=q0, sl=sl: e.scalar_tensor_tensor(
                            out=z_all[:, hp * TO + q0: hp * TO + q0 + 512],
                            in0=bank(4), scalar=float(2.0 ** -12),
                            in1=u_sb[sl][:, q0: q0 + 512],
                            op0=ALU.mult, op1=ALU.mult),
                            avb.rdeps() + ld_bufs["u_sb"][sl].rdeps()
                            + (z_buf[hp].wdeps() if qt == 0 else []), "avu")
                        avb.read(st)
                        ld_bufs["u_sb"][sl].read(st)
                        if qt == 0:
                            z_buf[hp].wrote(st)
                        else:
                            z_buf[hp].producer = st

            if "z" in dbg:
                for slot in range(8):
                    d = dma("ST", dbg["z"][slot * 128:(slot + 1) * 128, :],
                            z_all[:, slot * TO:(slot + 1) * TO],
                            z_buf[slot].rdeps(), "dbgz")
                    z_buf[slot].read(d)

            S.fence()
            with ExitStack() as S3:
                sb3 = lambda *a, **k: S3.enter_context(nc.sbuf_tensor(*a, **k))
                w2g = sb3("w2g", [128, 8 * D], BF16)
                zbf = sb3("zbf", [128, 8 * 512], BF16)
                zsq = sb3("zsq", [128, 8 * 512], BF16)
                P_sb = sb3("P_sb", [128, 8 * 512], F32)
                xt_sb = [sb3(f"xt_sb{i}", [128, D], F32) for i in range(2)]
                t0_sb = sb3("t0_sb", [128, D], F32)
                t1_sb = sb3("t1_sb", [128, D], F32)
                r_sb = sb3("r_sb", [128, D], F32)
                x1_sb = sb3("x1_sb", [128, D], F32)
                lt_sb = sb3("lt_sb", [128, D], F32)
                lr_sb = sb3("lr_sb", [128, D], F32)
                lu_sb = sb3("lu_sb", [128, D], F32)
                xq_sb = sb3("xq_sb", [128, D], BF16)
                stc_sb = sb3("stc_sb", [128, 8], F32)
                tmpbf = sb3("tmpbf", [128, D], BF16)
                bet_bf = sb3("bet_bf", [128, 8], BF16)
                c1_rep = sb3("c1_rep", [128, D], F32)
                c2_rep = sb3("c2_rep", [128, D], F32)
                crow = srow
                b2a_rs = sb3("b2a_rs", [1, D], F32)

                ld_b2ar = dma("SY", b2a_rs[:], din["b2a_r"], [])

                with ExitStack() as S3p:
                    sb3p = lambda *a, **k: S3p.enter_context(nc.sbuf_tensor(*a, **k))
                    w2st = sb3p("w2st", [128, 4 * D], F32)
                    w2st_buf = Buf()

                    cb1 = S.add("V", lambda e: e.tensor_copy(out=bet_bf[:], in_=bet_c[:]),
                                [ld_bet], "betbf")
                    w2g_ops = [None] * 8
                    tmp_buf = Buf()
                    c2b0, c2b1 = psum[6], psum[7]
                    mmc2 = []
                    for grp in range(2):
                        src_w2 = AP(tensor=din["w2a"].tensor, offset=grp * 4 * 128 * D,
                                    ap=[[D, 128], [128 * D, 4], [1, D]])
                        ld_w2 = dma("SY", w2st.ap().rearrange("p (c o) -> p c o", c=4),
                                    src_w2, w2st_buf.wdeps(), "w2ld")
                        w2st_buf.wrote(ld_w2)
                        for ci in range(4):
                            c = grp * 4 + ci
                            o = S.add("V", lambda e, c=c, ci=ci: e.tensor_scalar(
                                out=w2g[:, c * D:(c + 1) * D], in0=w2st[:, ci * D:(ci + 1) * D],
                                scalar1=gam_c[:, c: c + 1], scalar2=None, op0=ALU.mult),
                                [ld_w2, ld_gam], f"w2g{c}")
                            w2st_buf.read(o)
                            w2g_ops[c] = o
                            cast = S.add("V", lambda e, ci=ci: e.tensor_copy(
                                out=tmpbf[:], in_=w2st[:, ci * D:(ci + 1) * D]),
                                [ld_w2] + tmp_buf.wdeps(), "c2cast")
                            w2st_buf.read(cast)
                            tmp_buf.wrote(cast)
                            for nt in range(2):
                                first, last = c == 0, c == 7
                                bb = (c2b0, c2b1)[nt]
                                deps = tmp_buf.rdeps() + [cb1]
                                if first:
                                    deps += bb.wdeps()
                                mm = S.add("T", lambda e, nt=nt, c=c, f=first, l=last: e.matmul(
                                    bank(6 + nt)[0:1, :], bet_bf[:, c: c + 1],
                                    tmpbf[:, nt * 512:(nt + 1) * 512],
                                    start=f, stop=l), deps, "c2mm")
                                tmp_buf.read(mm)
                                if first:
                                    bb.wrote(mm)
                                else:
                                    bb.producer = mm
                                mmc2.append(mm)
                    evc2 = S.add("V", lambda e: e.tensor_add(
                        crow[0:1, D: 2 * D], PS[0:1, 6 * 512: 8 * 512], b2a_rs[:]),
                        [mmc2[-1], ld_b2ar], "c2ev")
                    c2b0.read(evc2); c2b1.read(evc2)

                    c1b0, c1b1 = psum[6], psum[7]
                    mmc1 = []
                    for c in range(8):
                        for nt in range(2):
                            first, last = c == 0, c == 7
                            bb = (c1b0, c1b1)[nt]
                            deps = [w2g_ops[c], op_ones]
                            if first:
                                deps += bb.wdeps() + [evc2]
                            mm = S.add("T", lambda e, nt=nt, c=c, f=first, l=last: e.matmul(
                                bank(6 + nt)[0:1, :], ones_bf[:, 0:1],
                                w2g[:, c * D + nt * 512: c * D + (nt + 1) * 512],
                                start=f, stop=l), deps, "c1mm")
                            if first:
                                bb.wrote(mm)
                            else:
                                bb.producer = mm
                            mmc1.append(mm)
                    evc1 = S.add("V", lambda e: e.tensor_copy(
                        out=crow[0:1, 0:D], in_=PS[0:1, 6 * 512: 8 * 512]),
                        [mmc1[-1]], "c1ev")
                    c1b0.read(evc1); c1b1.read(evc1)

                    spc = dma("ST", scr["row"][2:3, 0: 2 * D], crow[0:1, 0: 2 * D], [evc1, evc2], "cspill")
                    ld_c1 = bcast_load(c1_rep[:], scr["row"], 2 * H4, D, [spc], "c1rep")
                    ld_c2 = bcast_load(c2_rep[:], scr["row"], 2 * H4 + D, D, [spc], "c2rep")

                zbf_buf = Buf()
                zsq_buf = Buf()
                P_buf = Buf()
                stc_buf = Buf()
                row_bufs = [Buf() for _ in range(8)]
                xt_buf = [Buf(), Buf()]
                misc = {nm: Buf() for nm in ("t0", "t1", "r", "x1", "lt", "lr", "lu",
                                             "xq", "bn6", "bn2", "sq")}

                x1_spills = []
                xq_spills = []
                w1q_spill_ops = [[] for _ in range(8)]
                w2q_spill_ops = [[] for _ in range(32)]

                for qt in range(4):
                    q0 = qt * 512
                    casts = []
                    sqs = []
                    for slot in range(8):
                        cp = S.add("V", lambda e, slot=slot, q0=q0: e.tensor_copy(
                            out=zbf[:, slot * 512:(slot + 1) * 512],
                            in_=z_all[:, slot * TO + q0: slot * TO + q0 + 512]),
                            z_buf[slot].rdeps() + (zbf_buf.wdeps() if slot == 0 else []),
                            "zcast")
                        z_buf[slot].read(cp)
                        casts.append(cp)
                        sq = S.add("A", lambda e, slot=slot, q0=q0: e.activation(
                            out=zsq[:, slot * 512:(slot + 1) * 512],
                            in_=z_all[:, slot * TO + q0: slot * TO + q0 + 512],
                            func=ACT.Square),
                            z_buf[slot].rdeps() + (zsq_buf.wdeps() if slot == 0 else []),
                            "zsq")
                        z_buf[slot].read(sq)
                        sqs.append(sq)
                    zbf_buf.wrote(casts[-1])
                    zsq_buf.wrote(sqs[-1])

                    smz, smq = psum[6], psum[7]
                    mz = mq = None
                    for slot in range(8):
                        first, last = slot == 0, slot == 7
                        deps = [casts[slot], op_ones] + (smz.wdeps() if first else [])
                        mz = S.add("T", lambda e, slot=slot, f=first, l=last: e.matmul(
                            bank(6)[0:1, :], ones_bf[:, 0:1],
                            zbf[:, slot * 512:(slot + 1) * 512], start=f, stop=l),
                            deps, "szmm")
                        zbf_buf.read(mz)
                        if first:
                            smz.wrote(mz)
                        else:
                            smz.producer = mz
                        deps = [sqs[slot], op_ones] + (smq.wdeps() if first else [])
                        mq = S.add("T", lambda e, slot=slot, f=first, l=last: e.matmul(
                            bank(7)[0:1, :], ones_bf[:, 0:1],
                            zsq[:, slot * 512:(slot + 1) * 512], start=f, stop=l),
                            deps, "sqmm")
                        zsq_buf.read(mq)
                        if first:
                            smq.wrote(mq)
                        else:
                            smq.producer = mq

                    def rsl(i):
                        return srow[0:1, i * 512:(i + 1) * 512]
                    o1 = S.add("V", lambda e: e.tensor_scalar(
                        out=rsl(0), in0=bank(6)[0:1, :], scalar1=-1.0 / 1024.0,
                        scalar2=None, op0=ALU.mult),
                        [mz, ld_c1, ld_c2] + row_bufs[0].wdeps()
                        + ([spr1, spr2] if qt == 0 else []), "mu")
                    smz.read(o1); row_bufs[0].wrote(o1)
                    o2 = S.add("V", lambda e: e.tensor_scalar(
                        out=rsl(1), in0=bank(7)[0:1, :], scalar1=1.0 / 1024.0,
                        scalar2=None, op0=ALU.mult), [mq] + row_bufs[1].wdeps(), "msq")
                    smq.read(o2); row_bufs[1].wrote(o2)
                    o3 = S.add("V", lambda e: e.tensor_mul(rsl(2), rsl(0), rsl(0)),
                               [o1] + row_bufs[2].wdeps(), "musq")
                    row_bufs[2].wrote(o3)
                    o4 = S.add("V", lambda e: e.tensor_sub(rsl(3), rsl(1), rsl(2)),
                               [o2, o3] + row_bufs[3].wdeps(), "var")
                    row_bufs[3].wrote(o4)
                    o5 = S.add("A", lambda e: e.activation(
                        out=rsl(4), in_=rsl(3), func=ACT.Sqrt, bias=eps_sb[0:1, 1:2]),
                        [o4, op_eps] + row_bufs[4].wdeps(), "sqr")
                    row_bufs[4].wrote(o5)
                    o6 = S.add("V", lambda e: e.reciprocal(rsl(5), rsl(4)),
                               [o5] + row_bufs[5].wdeps(), "rstd")
                    row_bufs[5].wrote(o6)
                    o7 = S.add("V", lambda e: e.tensor_mul(rsl(6), rsl(0), rsl(5)),
                               [o6, o1] + row_bufs[6].wdeps(), "mr")
                    row_bufs[6].wrote(o7)

                    pev = []
                    for oc in range(8):
                        pb = psum[oc % 2]
                        mml = None
                        for c in range(8):
                            first, last = c == 0, c == 7
                            deps = [casts[c], w2g_ops[c]]
                            if first:
                                deps = deps + pb.wdeps()
                            mml = S.add("T", lambda e, oc=oc, c=c, f=first, l=last, b=oc % 2: e.matmul(
                                bank(b), w2g[:, c * D + oc * 128: c * D + oc * 128 + 128],
                                zbf[:, c * 512:(c + 1) * 512],
                                start=f, stop=l), deps, "pmm")
                            zbf_buf.read(mml)
                            if first:
                                pb.wrote(mml)
                            else:
                                pb.producer = mml
                        ev = S.add("V", lambda e, oc=oc, b=oc % 2: e.tensor_copy(
                            out=P_sb[:, oc * 512:(oc + 1) * 512], in_=bank(b)),
                            pb.rdeps() + P_buf.wdeps(), "pev")
                        pb.read(ev)
                        pev.append(ev)
                    P_buf.wrote(pev[-1])
                    if "P" in dbg:
                        d = dma("ST", dbg["P"][qt * 128:(qt + 1) * 128, :], P_sb[:],
                                [pev[-1]], "dbgP")
                        P_buf.read(d)

                    stb = psum[5]
                    trs = []
                    for tt in range(4):
                        for ri, rop in ((0, o6), (1, o7)):
                            tr = S.add("T", lambda e, tt=tt, ri=ri: e.transpose(
                                bank(5)[:, tt * 2 + ri: tt * 2 + ri + 1],
                                srow[0:1, (5 + ri) * 512 + tt * 128: (5 + ri) * 512 + (tt + 1) * 128],
                                identity[0:1, 0:1]),
                                [rop, ld_ident] + (stb.wdeps() if (tt == 0 and ri == 0) else []),
                                "sttr")
                            if tt == 0 and ri == 0:
                                stb.wrote(tr)
                            else:
                                stb.producer = tr
                            trs.append(tr)
                    evst = S.add("V", lambda e: e.tensor_copy(
                        out=stc_sb[:], in_=bank(5, 0, 8)),
                        [trs[-1]] + stc_buf.wdeps(), "stev")
                    stb.read(evst)
                    stc_buf.wrote(evst)
                    row_bufs[5].read(trs[-1])
                    row_bufs[6].read(trs[-1])

                    for tt in range(4):
                        yb0, yb1 = psum[2], psum[3]
                        ytr = []
                        for oc in range(8):
                            first = oc == 0
                            deps = [pev[oc], ld_ident]
                            if first:
                                deps = deps + yb0.wdeps() + yb1.wdeps()
                            tr = S.add("T", lambda e, oc=oc, tt=tt: e.transpose(
                                PS[0:128, 2 * 512 + oc * 128: 2 * 512 + (oc + 1) * 128],
                                P_sb[:, oc * 512 + tt * 128: oc * 512 + (tt + 1) * 128],
                                identity[:]), deps, "ytr")
                            P_buf.read(tr)
                            ytr.append(tr)
                        yb0.wrote(ytr[-1]); yb1.producer = ytr[-1]

                        xt = xt_sb[tt % 2]
                        xtb_ = xt_buf[tt % 2]
                        ldx = dma("SY", xt[:], din["x_own"][q0 + tt * 128: q0 + (tt + 1) * 128, :],
                                  xtb_.wdeps(), "ldx")
                        xtb_.wrote(ldx)

                        v1 = S.add("V", lambda e, xt=xt: e.tensor_add(t0_sb[:], xt[:], c2_rep[:]),
                                   [ldx, ld_c2] + misc["t0"].wdeps(), "t0")
                        xtb_.read(v1); misc["t0"].wrote(v1)
                        v2 = S.add("V", lambda e, tt=tt: e.scalar_tensor_tensor(
                            out=t1_sb[:], in0=c1_rep[:], scalar=stc_sb[:, tt * 2 + 1: tt * 2 + 2],
                            in1=t0_sb[:], op0=ALU.mult, op1=ALU.add),
                            [v1, ld_c1, evst] + misc["t1"].wdeps(), "t1")
                        stc_buf.read(v2); misc["t0"].read(v2); misc["t1"].wrote(v2)
                        v3 = S.add("V", lambda e, tt=tt: e.scalar_tensor_tensor(
                            out=r_sb[:], in0=PS[0:128, 2 * 512: 4 * 512],
                            scalar=stc_sb[:, tt * 2: tt * 2 + 1], in1=t1_sb[:],
                            op0=ALU.mult, op1=ALU.add),
                            [ytr[-1], v2] + misc["r"].wdeps(), "rtile")
                        yb0.read(v3); yb1.read(v3); stc_buf.read(v3)
                        misc["t1"].read(v3); misc["r"].wrote(v3)

                        s1 = S.add("A", lambda e: e.activation(
                            out=xq_sb[:], in_=r_sb[:], func=ACT.Identity,
                            accum_out=scol[:, 6:7]),
                            [v3] + misc["xq"].wdeps() + misc["bn6"].wdeps(), "ln2s")
                        s2 = S.add("A", lambda e: e.activation(
                            out=xq_sb[:], in_=r_sb[:], func=ACT.Square,
                            accum_out=scol[:, 7:8]),
                            [v3], "ln2q")
                        misc["bn6"].wrote(s2)
                        m1 = S.add("V", lambda e: e.tensor_scalar_mul(
                            scol[:, 2:3], scol[:, 6:7], 1.0 / 1024.0),
                            [s1, s2] + misc["bn2"].wdeps(), "ln2m1")
                        m2 = S.add("V", lambda e: e.tensor_mul(
                            scol[:, 3:4], scol[:, 2:3], scol[:, 2:3]), [], "ln2m2", force=[m1])
                        m3 = S.add("V", lambda e: e.tensor_scalar_mul(
                            scol[:, 4:5], scol[:, 7:8], 1.0 / 1024.0), [s2], "ln2m3")
                        m4 = S.add("V", lambda e: e.tensor_sub(
                            scol[:, 5:6], scol[:, 4:5], scol[:, 3:4]), [], "ln2m4", force=[m2, m3])
                        misc["bn2"].wrote(m4)
                        a1 = S.add("A", lambda e: e.activation(
                            out=scol[:, 0:1], in_=scol[:, 5:6], func=ACT.Sqrt, bias=eps_sb[:, 1:2]),
                            [m4, op_eps] + misc["sq"].wdeps(), "sq2")
                        misc["bn2"].read(a1); misc["sq"].wrote(a1)
                        r1 = S.add("V", lambda e: e.reciprocal(scol[:, 1:2], scol[:, 0:1]),
                                   [a1], "rc2")
                        misc["sq"].read(r1)
                        x1o = S.add("V", lambda e: e.tensor_scalar(
                            out=x1_sb[:], in0=r_sb[:], scalar1=scol[:, 2:3],
                            scalar2=scol[:, 1:2], op0=ALU.subtract, op1=ALU.mult),
                            [m4] + misc["x1"].wdeps(), "x1", force=[r1])
                        misc["r"].read(x1o); misc["x1"].wrote(x1o)
                        misc["xq"].read(x1o)

                        q1 = S.add("V", lambda e: e.tensor_reduce(
                            out=scol[:, 2:3], in_=x1_sb[:], axis=AX, op=ALU.max,
                            apply_absolute_value=True), [x1o], "xsr")
                        q2 = S.add("V", lambda e: e.tensor_scalar_max(
                            scol[:, 3:4], scol[:, 2:3], QEPS), [], "xsc", force=[q1])
                        q4 = S.add("V", lambda e: e.reciprocal(scol[:, 4:5], scol[:, 3:4]),
                                   [], "xsi", force=[q2])
                        q5 = S.add("V", lambda e: e.tensor_scalar_mul(
                            scol[:, 5:6], scol[:, 4:5], 127.0), [], "s127", force=[q4])
                        q3 = S.add("V", lambda e, qt=qt, tt=tt: e.tensor_copy(
                            out=xs_all[:, qt * 4 + tt: qt * 4 + tt + 1], in_=scol[:, 3:4]),
                            [], "xssave", force=[q2])

                        l1 = S.add("V", lambda e: e.tensor_scalar(
                            out=lt_sb[:], in0=x1_sb[:], scalar1=scol[:, 5:6],
                            scalar2=128.5, op0=ALU.mult, op1=ALU.add),
                            [q3, x1o] + misc["lt"].wdeps(), "lad0", force=[q5])
                        misc["x1"].read(l1); misc["lt"].wrote(l1)
                        lc = S.add("V", lambda e: e.tensor_copy(out=lr_sb[:], in_=lt_sb[:]),
                                   [l1] + misc["lr"].wdeps(), "ladc")
                        misc["lr"].wrote(lc)
                        prev = lc
                        for bbit in (128.0, 64.0, 32.0, 16.0, 8.0, 4.0, 2.0, 1.0):
                            pu = S.add("V", lambda e, bb=bbit: e.tensor_scalar(
                                out=lu_sb[:], in0=lr_sb[:], scalar1=bb, scalar2=bb,
                                op0=ALU.is_ge, op1=ALU.mult), [prev], "ladg")
                            prev = S.add("V", lambda e: e.tensor_sub(lr_sb[:], lr_sb[:], lu_sb[:]),
                                         [pu], "lads")
                        xqo = S.add("V", lambda e: e.scalar_tensor_tensor(
                            out=xq_sb[:], in0=lt_sb[:], scalar=-128.0, in1=lr_sb[:],
                            op0=ALU.add, op1=ALU.subtract),
                            [prev] + misc["xq"].wdeps(), "xqf")
                        misc["lt"].read(xqo); misc["xq"].wrote(xqo)

                        row0 = q0 + tt * 128
                        sp1 = dma("ST", scr["x1"][row0: row0 + 128, :], x1_sb[:], [x1o], "x1sp")
                        misc["x1"].read(sp1)
                        x1_spills.append(sp1)
                        sp2 = dma("ST", scr["xq"][row0: row0 + 128, :], xq_sb[:], [xqo], "xqsp")
                        misc["xq"].read(sp2)
                        xq_spills.append(sp2)
                        if "x1" in dbg:
                            d = dma("ST", dbg["x1"][row0: row0 + 128, :], x1_sb[:], [x1o], "dbgx1")
                            misc["x1"].read(d)
                        if "r" in dbg:
                            d = dma("ST", dbg["r"][row0: row0 + 128, :], r_sb[:], [v3], "dbgr")
                            misc["r"].read(d)

                    for j in range(16):
                        if j < 8:
                            c = 2 * qt + j // 4
                            s0 = (j % 4) * D
                            src = din["w1f_T"][c * 128:(c + 1) * 128, s0: s0 + D]
                            dst = scr["w1q"][c * 128:(c + 1) * 128, s0: s0 + D]
                            tp, tn = ld_th1, ld_th1n
                            splist = w1q_spill_ops[c]
                        else:
                            c = 8 * qt + (j - 8)
                            src = din["w2f_T"][c * 128:(c + 1) * 128, :]
                            dst = scr["w2q"][c * 128:(c + 1) * 128, :]
                            tp, tn = ld_th2, ld_th2n
                            splist = w2q_spill_ops[c]
                        cp_, cn_ = (10, 11) if j < 8 else (18, 19)
                        xt = xt_sb[j % 2]
                        xtb_ = xt_buf[j % 2]
                        ld = dma(("G", "ST")[j % 2], xt[:], src, xtb_.wdeps(),
                                 f"wtld{qt}_{j}")
                        xtb_.wrote(ld)
                        ta = S.add("V", lambda e, xt=xt, cp=cp_: e.tensor_single_scalar(
                            out=lt_sb[:], in_=xt[:], scalar=scol[:, cp: cp + 1],
                            op=ALU.is_ge),
                            [ld, tp] + misc["lt"].wdeps(), "tern_a")
                        xtb_.read(ta); misc["lt"].wrote(ta)
                        tb_ = S.add("V", lambda e, xt=xt, cn=cn_: e.tensor_single_scalar(
                            out=lr_sb[:], in_=xt[:], scalar=scol[:, cn: cn + 1],
                            op=ALU.is_le),
                            [ld, tn] + misc["lr"].wdeps(), "tern_b")
                        xtb_.read(tb_); misc["lr"].wrote(tb_)
                        tq_ = S.add("V", lambda e: e.tensor_sub(tmpbf[:], lt_sb[:], lr_sb[:]),
                                    [ta, tb_] + tmp_buf.wdeps(), "tern_q")
                        misc["lt"].read(tq_); misc["lr"].read(tq_)
                        tmp_buf.wrote(tq_)
                        sp = dma("G", dst, tmpbf[:], [tq_], f"wtsp{qt}_{j}")
                        tmp_buf.read(sp)
                        splist.append(sp)

        S.fence()
        g_spills = []
        q1_ops = []
        with ExitStack() as S5a:
            sb5 = lambda *a, **k: S5a.enter_context(nc.sbuf_tensor(*a, **k))
            w1q_sb = sb5("w1q_sb", [128, 8 * H4], BF16)
            xqT_sb = sb5("xqT_sb", [128, 8 * TO], BF16, is_transpose_dest=True)
            b1_rep = sb5("b1_rep", [128, H4], F32)
            t_sb = sb5("t_sb", [128, H4], F32)
            ga_sb = [sb5(f"ga_sb{i}", [128, H4], F32) for i in range(2)]
            ga_buf = [Buf(), Buf()]
            t_buf = Buf()

            w1q_lds = []
            for c in range(8):
                ld = dma("G", w1q_sb[:, c * H4:(c + 1) * H4],
                         scr["w1q"][c * 128:(c + 1) * 128, :],
                         list(w1q_spill_ops[c]), f"w1qld{c}")
                w1q_lds.append(ld)
            ld_b1rep = bcast_load(b1_rep[:], din["b1f_r"], 0, H4, [], "b1rep")
            xqt_lds = []
            for kc in range(8):
                t = dma_t(xqT_sb[:, kc * TO:(kc + 1) * TO],
                          scr["xq"][:, kc * 128:(kc + 1) * 128],
                          list(xq_spills), "xqT")
                xqt_lds.append(t)
            d1o = S.add("V", lambda e: e.tensor_scalar(
                out=d1_all[:], in0=xs_all[:], scalar1=scol[:, 12:13], scalar2=None,
                op0=ALU.mult), [ld_d1f] + [x1_spills[-1]], "d1all")

            hps = [psum[i] for i in range(8)]
            for tt in range(16):
                mms_ntg = []
                for ntg in range(2):
                    mml = []
                    for kc in range(8):
                        for nt in range(4):
                            bi = ntg * 4 + nt
                            bb = hps[bi]
                            first = kc == 0
                            deps = [xqt_lds[kc], w1q_lds[kc]]
                            if first:
                                deps += bb.wdeps()
                            mm = S.add("T", lambda e, kc=kc, bi=bi, tt=tt: e.matmul(
                                bank(bi), xqT_sb[:, kc * TO + tt * 128: kc * TO + (tt + 1) * 128],
                                w1q_sb[:, kc * H4 + bi * 512: kc * H4 + (bi + 1) * 512],
                                start=(kc == 0), stop=(kc == 7)), deps, "h1mm")
                            if first:
                                bb.wrote(mm)
                            else:
                                bb.producer = mm
                            mml.append(mm)
                    mms_ntg.append(mml)
                evs = []
                for ntg in range(2):
                    deps = [mms_ntg[ntg][-1], d1o, ld_b1rep]
                    ev = S.add("V", lambda e, ntg=ntg, tt=tt: e.scalar_tensor_tensor(
                        out=t_sb[:, ntg * 2048:(ntg + 1) * 2048],
                        in0=PS[0:128, ntg * 4 * 512:(ntg + 1) * 4 * 512],
                        scalar=d1_all[:, tt: tt + 1],
                        in1=b1_rep[:, ntg * 2048:(ntg + 1) * 2048],
                        op0=ALU.mult, op1=ALU.add),
                        deps + (t_buf.wdeps() if ntg == 0 else []), "h1ev")
                    for nt in range(4):
                        hps[ntg * 4 + nt].read(ev)
                    evs.append(ev)
                t_buf.wrote(evs[-1])
                ga = ga_sb[tt % 2]
                gab = ga_buf[tt % 2]
                gl = S.add("A", lambda e, ga=ga: e.activation(
                    out=ga[:], in_=t_sb[:], func=ACT.Gelu),
                    [evs[0], evs[1]] + gab.wdeps(), "gelu")
                t_buf.read(gl); gab.wrote(gl)
                q1 = S.add("V", lambda e, ga=ga, tt=tt: e.tensor_reduce(
                    out=gs_all[:, tt: tt + 1], in_=ga[:], axis=AX, op=ALU.max,
                    apply_absolute_value=True), [gl], "gsr")
                gab.read(q1)
                q1_ops.append(q1)
                sp = dma("ST", scr["g"][tt * 128:(tt + 1) * 128, :], ga[:], [gl], "gsp")
                gab.read(sp)
                g_spills.append(sp)

        S.fence()
        out_stores = []
        with ExitStack() as S5b:
            sb6 = lambda *a, **k: S5b.enter_context(nc.sbuf_tensor(*a, **k))
            w2q_sb = sb6("w2q_sb", [128, 32 * D], BF16)
            gt_sb = [sb6(f"gt_sb{i}", [128, H4], F32) for i in range(2)]
            lt2 = sb6("lt2", [128, H4], F32)
            lr2 = sb6("lr2", [128, H4], F32)
            gq_sb = sb6("gq_sb", [128, H4], BF16)
            gqT_sb = sb6("gqT_sb", [128, 32 * 256], BF16, is_transpose_dest=True)
            x1t = [sb6(f"x1t{i}", [128, D], F32) for i in range(2)]
            rb = sb6("rb", [128, D], F32)
            wjunk2 = sb6("wjunk2", [128, D], BF16)
            b2f_rep = sb6("b2f_rep", [128, D], F32)
            gt_buf = [Buf(), Buf()]
            lt_buf = Buf(); lr_buf = Buf(); gq_buf = Buf(); gqt_buf = Buf()
            x1t_buf = [Buf(), Buf()]
            m6 = {nm: Buf() for nm in ("r", "bn6", "bn2", "sq")}

            w2q_lds = []
            for c in range(32):
                ld = dma("G", w2q_sb[:, c * D:(c + 1) * D],
                         scr["w2q"][c * 128:(c + 1) * 128, :],
                         list(w2q_spill_ops[c]), f"w2qld{c}")
                w2q_lds.append(ld)
            ld_b2frep = bcast_load(b2f_rep[:], din["b2f_r"], 0, D, [], "b2frep")

            c1o = S.add("V", lambda e: e.tensor_scalar_max(gs_all[:], gs_all[:], QEPS),
                        [], "gclip", force=[q1_ops[-1]])
            c2o = S.add("V", lambda e: e.reciprocal(g127_all[:], gs_all[:]),
                        [], "ginv", force=[c1o])
            c3o = S.add("V", lambda e: e.tensor_scalar_mul(g127_all[:], g127_all[:], 127.0),
                        [], "g127", force=[c2o])
            d2o = S.add("V", lambda e: e.tensor_scalar(
                out=d2_all[:], in0=gs_all[:], scalar1=scol[:, 20:21], scalar2=None,
                op0=ALU.mult), [ld_d2f], "d2all", force=[c1o])

            g_lds = {}

            def g_load(tt):
                d = dma("SY", gt_sb[tt % 2][:], scr["g"][tt * 128:(tt + 1) * 128, :],
                        [g_spills[tt]] + gt_buf[tt % 2].wdeps(), f"gld{tt}")
                gt_buf[tt % 2].wrote(d)
                g_lds[tt] = d

            g_load(0)
            g_load(1)

            gq_spill_l = {}

            def ladder_tile(tt):
                gt = gt_sb[tt % 2]
                gb = gt_buf[tt % 2]
                l1 = S.add("A", lambda e, gt=gt, tt=tt: e.activation(
                    out=lt2[:], in_=gt[:], func=ACT.Identity,
                    scale=g127_all[:, tt: tt + 1], bias=h128_sb[:]),
                    [g_lds[tt], c3o, op_h128] + lt_buf.wdeps(), "g_l0")
                gb.read(l1); lt_buf.wrote(l1)
                pu = S.add("V", lambda e, gt=gt: e.tensor_scalar(
                    out=gt[:], in0=lt2[:], scalar1=128.0, scalar2=128.0,
                    op0=ALU.is_ge, op1=ALU.mult), [l1] + gb.wdeps(), "g_lg")
                gb.wrote(pu)
                prev = S.add("V", lambda e, gt=gt: e.tensor_sub(lr2[:], lt2[:], gt[:]),
                             [pu] + lr_buf.wdeps(), "g_ls")
                gb.read(prev)
                for bbit in (64.0, 32.0, 16.0, 8.0, 4.0, 2.0, 1.0):
                    pu = S.add("V", lambda e, bb=bbit, gt=gt: e.tensor_scalar(
                        out=gt[:], in0=lr2[:], scalar1=bb, scalar2=bb,
                        op0=ALU.is_ge, op1=ALU.mult), [prev], "g_lg")
                    gb.wrote(pu)
                    prev = S.add("V", lambda e, gt=gt: e.tensor_sub(lr2[:], lr2[:], gt[:]),
                                 [pu], "g_ls")
                    gb.read(prev)
                lr_buf.wrote(prev)
                gqo = S.add("V", lambda e: e.scalar_tensor_tensor(
                    out=gq_sb[:], in0=lt2[:], scalar=-128.0, in1=lr2[:],
                    op0=ALU.add, op1=ALU.subtract), [prev] + gq_buf.wdeps(), "gqf")
                lt_buf.read(gqo); lr_buf.read(gqo); gq_buf.wrote(gqo)
                sp = dma("ST", scr["gq"][tt * 128:(tt + 1) * 128, :], gq_sb[:], [gqo], "gqsp")
                gq_buf.read(sp)
                gq_spill_l[tt] = sp
                if tt + 2 < 16:
                    g_load(tt + 2)

            ldx_gates = {}

            def ldx_prefetch(tt):
                xt = x1t[tt % 2]
                xb_ = x1t_buf[tt % 2]
                ldx = dma("SY", xt[:], scr["x1"][tt * 128:(tt + 1) * 128, :],
                          xb_.wdeps() + [x1_spills[tt]], "x1ld")
                ldg = dma("SY", syjunk[:], din["ident"][:, 2:4], [ldx], "x1gate")
                xb_.wrote(ldx)
                ldx_gates[tt] = ldg

            def emit_group(grp):
                base = (grp % 2) * 4
                ladder_tile(grp * 2)
                if grp >= 1:
                    ldx_prefetch((grp - 1) * 2)
                    ldx_prefetch((grp - 1) * 2 + 1)
                ladder_tile(grp * 2 + 1)
                spgate = dma("ST", scr["col"][3:4, 0:2], stjunk[:],
                             [gq_spill_l[grp * 2], gq_spill_l[grp * 2 + 1], op_stj],
                             "gqgate")
                glds = []
                for hc in range(32):
                    deps = [spgate]
                    if hc == 0:
                        deps += gqt_buf.wdeps()
                    t = dma_t(gqT_sb[:, hc * 256:(hc + 1) * 256],
                              scr["gq"][grp * 256:(grp + 1) * 256, hc * 128:(hc + 1) * 128],
                              deps, "gqT")
                    glds.append(t)
                gqt_buf.wrote(glds[-1])
                mml_last = [None, None]
                for tl in range(2):
                    for hc in range(32):
                        for nt in range(2):
                            bi = base + tl * 2 + nt
                            bb = psum[bi]
                            first = hc == 0
                            deps = [glds[hc], w2q_lds[hc]]
                            if first:
                                deps += bb.wdeps()
                            mm = S.add("T", lambda e, hc=hc, tl=tl, nt=nt, bi=bi: e.matmul(
                                bank(bi),
                                gqT_sb[:, hc * 256 + tl * 128: hc * 256 + (tl + 1) * 128],
                                w2q_sb[:, hc * D + nt * 512: hc * D + (nt + 1) * 512],
                                start=(hc == 0), stop=(hc == 31)), deps, "h2mm")
                            gqt_buf.read(mm)
                            if first:
                                bb.wrote(mm)
                            else:
                                bb.producer = mm
                            mml_last[tl] = mm

                def epilogue(grp=grp, base=base, mml_last=mml_last):
                    for tl in range(2):
                        tt = grp * 2 + tl
                        xt = x1t[tt % 2]
                        xb_ = x1t_buf[tt % 2]
                        v1 = S.add("V", lambda e, xt=xt: e.tensor_add(rb[:], xt[:], b2f_rep[:]),
                                   [ldx_gates[tt], ld_b2frep] + m6["r"].wdeps(), "o_t0")
                        xb_.read(v1); m6["r"].wrote(v1)
                        bb0, bb1 = psum[base + tl * 2], psum[base + tl * 2 + 1]
                        v2 = S.add("V", lambda e, base=base, tl=tl, tt=tt: e.scalar_tensor_tensor(
                            out=rb[:], in0=PS[0:128, (base + tl * 2) * 512:(base + tl * 2 + 2) * 512],
                            scalar=d2_all[:, tt: tt + 1], in1=rb[:],
                            op0=ALU.mult, op1=ALU.add),
                            [mml_last[tl], v1, d2o], "o_r")
                        bb0.read(v2); bb1.read(v2)
                        m6["r"].wrote(v2)
                        s1 = S.add("A", lambda e: e.activation(
                            out=wjunk2[:], in_=rb[:], func=ACT.Identity,
                            accum_out=scol[:, 9:10]),
                            [v2] + m6["bn6"].wdeps(), "ln3s")
                        s2 = S.add("A", lambda e: e.activation(
                            out=wjunk2[:], in_=rb[:], func=ACT.Square,
                            accum_out=scol[:, 13:14]),
                            [v2], "ln3q")
                        m6["bn6"].wrote(s2)
                        m1 = S.add("V", lambda e: e.tensor_scalar_mul(
                            scol[:, 14:15], scol[:, 9:10], 1.0 / 1024.0),
                            [s1, s2] + m6["bn2"].wdeps(), "ln3m1")
                        m2 = S.add("V", lambda e: e.tensor_mul(
                            scol[:, 15:16], scol[:, 14:15], scol[:, 14:15]), [], "ln3m2", force=[m1])
                        m3 = S.add("V", lambda e: e.tensor_scalar_mul(
                            scol[:, 23:24], scol[:, 13:14], 1.0 / 1024.0), [s2], "ln3m3")
                        m4 = S.add("V", lambda e: e.tensor_sub(
                            scol[:, 23:24], scol[:, 23:24], scol[:, 15:16]), [], "ln3m4", force=[m2, m3])
                        m6["bn2"].wrote(m4)
                        a1 = S.add("A", lambda e: e.activation(
                            out=scol[:, 21:22], in_=scol[:, 23:24], func=ACT.Sqrt, bias=eps_sb[:, 1:2]),
                            [m4, op_eps] + m6["sq"].wdeps(), "o_sq")
                        m6["bn2"].read(a1); m6["sq"].wrote(a1)
                        r1 = S.add("V", lambda e: e.reciprocal(scol[:, 22:23], scol[:, 21:22]),
                                   [a1], "o_rc")
                        m6["sq"].read(r1)
                        oo = S.add("V", lambda e, xt=xt: e.tensor_scalar(
                            out=xt[:], in0=rb[:], scalar1=scol[:, 14:15], scalar2=scol[:, 22:23],
                            op0=ALU.subtract, op1=ALU.mult),
                            [m4] + xb_.wdeps(), "o_out", force=[r1])
                        m6["r"].read(oo); xb_.wrote(oo)
                        st = dma("ST", y_out[tt * 128:(tt + 1) * 128, :], xt[:], [oo], "ystore")
                        xb_.read(st)
                        out_stores.append(st)
                return epilogue

            pend = None
            for grp in range(8):
                nxt = emit_group(grp)
                if pend is not None:
                    pend()
                pend = nxt
            ldx_prefetch(14)
            ldx_prefetch(15)
            pend()

        S.add("T", lambda e: e.nop(), out_stores, "final")

        S.finalize()
        S.emit(nc, sems)
    return nc


DEBUG_SPECS = {
    "z": ([16 * 64, TO], F32),
    "x1": ([TO, D], F32),
    "r": ([TO, D], F32),
    "P": ([4 * 128, 8 * 512], F32),
}



def make_in_maps(L0_attn_x, L0_attn_W1, L0_attn_b1, L0_attn_W2, L0_attn_b2,
                 L0_attn_gamma, L0_attn_beta, L0_ffn_W1, L0_ffn_b1, L0_ffn_W2,
                 L0_ffn_b2):
    f = np.float32
    c = np.ascontiguousarray
    w1a = c(L0_attn_W1, dtype=f)
    b1a = c(L0_attn_b1, dtype=f)
    w2a = c(L0_attn_W2, dtype=f)
    b2a = c(L0_attn_b2, dtype=f)
    gam = c(L0_attn_gamma, dtype=f)
    bet = c(L0_attn_beta, dtype=f)
    w1f = c(L0_ffn_W1, dtype=f)
    b1f = c(L0_ffn_b1, dtype=f)
    w2f = c(L0_ffn_W2, dtype=f)
    b2f = c(L0_ffn_b2, dtype=f)

    shared = {
        "w1a": w1a,
        "b1a_c": c(b1a.reshape(32, 128).T),
        "b1a_r": c(b1a.reshape(1, H4)),
        "w2a": w2a,
        "gam_c": c(gam.reshape(8, 128).T),
        "bet_c": c(bet.reshape(8, 128).T),
        "b2a_r": c(b2a.reshape(1, D)),
        "w1f_T": c(w1f.T),
        "b1f_r": c(b1f.reshape(1, H4)),
        "w2f_T": c(w2f.T),
        "b2f_r": c(b2f.reshape(1, D)),
        "ident": np.eye(128, dtype=f),
    }
    x = np.asarray(L0_attn_x, dtype=f)
    in_maps = []
    for core in range(8):
        b, h = core // 2, core % 2
        own = c(x[b, h * TO:(h + 1) * TO, :])
        oth = c(x[b, (1 - h) * TO:(2 - h) * TO, :])
        m = dict(shared)
        m["x_own"] = own
        m["xT_own"] = c(own.T)
        m["xT_oth"] = c(oth.T)
        in_maps.append(m)
    return in_maps


def kernel(**inputs):
    nc = build_program()
    in_maps = make_in_maps(**inputs)
    res = run_bass_kernel_spmd(nc, in_maps, core_ids=list(range(8)))
    out = np.empty((4, FS, D), dtype=np.float32)
    for core in range(8):
        b, h = core // 2, core % 2
        out[b, h * TO:(h + 1) * TO, :] = res.results[core]["y"]
    return out
